# version 09
# baseline (speedup 1.0000x reference)
import numpy as np
import contextlib
import concourse.bass as bass
import concourse.mybir as mybir
from concourse.bass_utils import run_bass_kernel_spmd

F32 = mybir.dt.float32
BF16 = mybir.dt.bfloat16
AL = mybir.AluOpType
AF = mybir.ActivationFunctionType

D = 2048
NK = 16
T = 2048
TS = 1024
NSUB = 2
FFC = 44
NG = 2
INV_D = float(np.float32(1.0 / D))
TWOPI = float(np.float32(2 * np.pi))
PI = float(np.pi)

ENGINES = ("tensor", "vector", "scalar", "sync", "gpsimd")


class SemC:
    def __init__(self, h):
        self.h = h
        self.v = 0


def build_nc():
    nc = bass.Bass("TRN2", target_bir_lowering=False)

    xT = nc.dram_tensor("xT", [D, T], F32, kind="ExternalInput")
    xNB = nc.dram_tensor("xNB", [D, T], F32, kind="ExternalInput")
    wv_p = nc.dram_tensor("wv_p", [NK, NK, 128, 128], F32, kind="ExternalInput")
    wo_p = nc.dram_tensor("wo_p", [NK, NK, 128, 128], F32, kind="ExternalInput")
    gate_p = nc.dram_tensor("gate_p", [FFC, NK, 128, 128], F32, kind="ExternalInput")
    up_p = nc.dram_tensor("up_p", [FFC, NK, 128, 128], F32, kind="ExternalInput")
    down_p = nc.dram_tensor("down_p", [NK, FFC, 128, 128], F32, kind="ExternalInput")
    Bcat = nc.dram_tensor("Bcat", [NK, 128, 512], F32, kind="ExternalInput")
    Ccat = nc.dram_tensor("Ccat", [4, 128, D], F32, kind="ExternalInput")
    vparams = nc.dram_tensor("vparams", [128, 64], F32, kind="ExternalInput")
    stats_w = nc.dram_tensor("stats_w", [128, 32], F32, kind="ExternalInput")
    lru_in = nc.dram_tensor("lru_in", [128, 4], F32, kind="ExternalInput")
    brouter = nc.dram_tensor("brouter", [1, 1], F32, kind="ExternalInput")
    cmask = nc.dram_tensor("cmask", [128, 1], F32, kind="ExternalInput")
    out_d = nc.dram_tensor("out", [D, T], F32, kind="ExternalOutput")
    hsp = nc.dram_tensor("hspill", [D, T], F32)
    bscr = nc.dram_tensor("bscr", [1, T], F32)
    bscr8 = nc.dram_tensor("bscr8", [1, T], mybir.dt.uint8)

    m_bc = nc.alloc_sbuf_tensor("m_bc", [128, T], F32)
    r_bc = nc.alloc_sbuf_tensor("r_bc", [128, T], F32)
    rr1_bc = nc.alloc_sbuf_tensor("rr1_bc", [128, T], F32)
    mask_bc = nc.alloc_sbuf_tensor("mask_bc", [128, T], mybir.dt.uint8)
    g_bf = nc.alloc_sbuf_tensor("g_bf", [128, 4 * T], BF16)
    R_bf = nc.alloc_sbuf_tensor("R_bf", [128, 4 * T], BF16)
    vp = nc.alloc_sbuf_tensor("vp", [128, 64], F32)
    small = nc.alloc_sbuf_tensor("small", [128, 32], F32)
    ones1 = nc.alloc_sbuf_tensor("ones1", [1, 128], F32)
    ones_bf = nc.alloc_sbuf_tensor("ones_bf", [128, 1], BF16)
    negones_bf = nc.alloc_sbuf_tensor("negones_bf", [128, 1], BF16)
    cmask_sb = nc.alloc_sbuf_tensor("cmask_sb", [128, 1], F32)
    br_sb = nc.alloc_sbuf_tensor("br_sb", [1, 1], F32)
    sw_sb = nc.alloc_sbuf_tensor("sw_sb", [128, 32], F32)

    AR_F32 = 36352
    arena = nc.alloc_sbuf_tensor("arena", [128, AR_F32], F32)
    arena_bf = arena.ap().bitcast(BF16)

    def arF(off_kb, cols):
        o = int(off_kb * 256)
        assert o + cols <= AR_F32, (off_kb, cols)
        return arena[:, o:o + cols]

    def arB(off_kb, cols):
        o = int(off_kb * 512)
        assert o + cols <= 2 * AR_F32, (off_kb, cols)
        return arena_bf[:, o:o + cols]

    ps = [nc.alloc_psum_tensor("ps%d" % i, [128, 512], F32) for i in range(8)]

    es = contextlib.ExitStack()
    def mksem(name):
        return SemC(es.enter_context(nc.semaphore(name)))
    sSm = mksem("sSm")
    sBc = mksem("sBc")
    sX = [mksem("sX0"), mksem("sX1")]
    sH = [mksem("sH0"), mksem("sH1")]
    sWv = [mksem("sWv0"), mksem("sWv1")]
    sUp = [mksem("sUp0"), mksem("sUp1")]
    sDn = [mksem("sDn0"), mksem("sDn1")]
    sO = mksem("sO")
    sBr = mksem("sBr")
    sOut = [mksem("sOut0"), mksem("sOut1")]
    sT = mksem("sT")
    sV = mksem("sV")
    sS = mksem("sS")
    sG = mksem("sG")
    ENG_SEM = {"tensor": sT, "vector": sV, "scalar": sS, "gpsimd": sG}
    DMA_SEMS = (sSm, sBc, sX[0], sX[1], sH[0], sH[1], sWv[0], sWv[1],
                sUp[0], sUp[1], sDn[0], sDn[1], sO, sBr, sOut[0], sOut[1])

    prog = {e: [] for e in ENGINES}
    phase_marks = []

    def pmark(name):
        phase_marks.append((name, len(prog["tensor"])))
    build_nc.phase_marks = phase_marks
    lastw = {}

    dirty = {e: False for e in ENGINES}

    def em(e, fn):
        prog[e].append(fn)

    def drain(e):
        if dirty[e]:
            prog[e].append(lambda E: E.drain())
            dirty[e] = False

    def wait(e, s, v=None):
        v = s.v if v is None else v
        if not v or v <= 0:
            return
        key = (e, id(s))
        if lastw.get(key, -1) >= v:
            return
        lastw[key] = v
        em(e, lambda E, h=s.h, vv=v: E.wait_ge(h, vv))

    def op(e, fn, inc=None, n=1, is_dma=False):
        if e in ("vector", "scalar", "gpsimd") and not is_dma:
            drain(e)
            dirty[e] = True
        if inc is None:
            em(e, lambda E, f=fn: f(E))
            return None
        inc.v += n
        mk = inc.v
        em(e, lambda E, f=fn, h=inc.h, nn=n: f(E).then_inc(h, nn))
        return mk

    def dma(queue, sem, out_ap, in_ap):
        return op(queue, lambda E, o=out_ap, i=in_ap: E.dma_start(out=o, in_=i),
                  inc=sem, n=16, is_dma=True)

    def mm(out_ap, lhs_ap, rhs_ap, start, stop, inc=None):
        return op("tensor",
                  lambda E, o=out_ap, l=lhs_ap, r=rhs_ap, a=start, b=stop:
                  E.matmul(o, l, r, start=a, stop=b), inc=inc)

    def tt(e, out_ap, a_ap, b_ap, alu, inc=None):
        return op(e, lambda E, o=out_ap, a=a_ap, b=b_ap, al=alu:
                  E.tensor_tensor(o, a, b, al), inc=inc)

    def tsc(e, out_ap, a_ap, s1, alu, inc=None):
        return op(e, lambda E, o=out_ap, a=a_ap, s=s1, al=alu:
                  E.tensor_single_scalar(o, a, s, al), inc=inc)

    def ts2(e, out_ap, a_ap, s1, s2, op0, op1, inc=None):
        return op(e, lambda E, o=out_ap, a=a_ap, x=s1, y=s2, o0=op0, o1=op1:
                  E.tensor_scalar(o, a, x, y, o0, o1), inc=inc)

    def stt(e, out_ap, a_ap, sc, b_ap, op0, op1, inc=None):
        return op(e, lambda E, o=out_ap, a=a_ap, s=sc, b=b_ap, o0=op0, o1=op1:
                  E.scalar_tensor_tensor(o, a, s, b, o0, o1), inc=inc)

    def act(out_ap, in_ap, func, bias=0.0, scale=1.0, inc=None):
        return op("scalar",
                  lambda E, o=out_ap, i=in_ap, f=func, b=bias, s=scale:
                  E.activation(o, i, f, bias=b, scale=s), inc=inc)

    def cp(e, out_ap, in_ap, inc=None):
        return op(e, lambda E, o=out_ap, i=in_ap: E.tensor_copy(o, i), inc=inc)

    def sel(out_ap, m_ap, a_ap, b_ap, inc=None):
        return op("vector", lambda E, o=out_ap, m=m_ap, a=a_ap, b=b_ap:
                  E.select(o, m, a, b, add_drain=True), inc=inc)

    def milestone(e):
        s = ENG_SEM[e]
        s.v += 1
        mk = s.v
        em(e, lambda E, h=s.h: E.drain().then_inc(h, 1))
        dirty[e] = False
        return mk

    def fence(writers=ENGINES, skip=()):
        marks = {e: milestone(e) for e in ("tensor", "vector", "scalar", "gpsimd")}
        for e in writers:
            for pe, mk in marks.items():
                if pe != e and (e, pe) not in skip:
                    wait(e, ENG_SEM[pe], mk)
            for dsem in DMA_SEMS:
                wait(e, dsem, dsem.v)

    C_NU, C_TH, C_MAG, C_COS, C_SIN = 0, 2, 4, 6, 8
    C_S1, C_S2 = 10, 12
    C_GI = 16
    C_BS = 20
    C_T1 = 24
    C_GAM = 26

    def sm(c, w=2):
        return small[:, c:c + w]

    def smc(c):
        return small[:, c:c + 1]

    pmark("P0a")
    dma("sync", sSm, vp[:, :], vparams.ap())
    dma("sync", sSm, small[:, 0:4], lru_in.ap())
    dma("sync", sSm, sw_sb[:, :], stats_w.ap())
    dma("sync", sSm, br_sb[:, :], brouter.ap())
    mk_small = dma("sync", sSm, cmask_sb[:, :], cmask.ap())
    op("vector", lambda E: E.memset(ones1[:, :], 1.0))
    op("vector", lambda E: E.memset(ones_bf[:, :], 1.0))
    mk_ones = op("vector", lambda E: E.memset(negones_bf[:, :], -1.0), inc=sV)

    bcat_sb = arB(96, NK * 512)
    mk_bcat = dma("gpsimd", sBc, bcat_sb.rearrange("p (k c) -> p k c", k=NK),
                  Bcat.ap().rearrange("k p c -> p k c"))

    xb = [arF(64 + 8 * i, T) for i in range(2)]
    xbf = [arB(80 + 4 * i, T) for i in range(2)]
    xsq = [arB(88 + 4 * i, T) for i in range(2)]
    row2 = arF(32, T)
    row_S1 = arF(40, T)
    row_sx2 = arF(48, T)
    row_tmp = arF(56, T)

    xslot_v = {}
    xslot_t = {}

    def stats_pass(src, own):
        mkv = {}
        mkt = {}
        for k in range(NK):
            sl = k % 2
            wait("sync", sV, xslot_v.get(sl))
            wait("sync", sT, xslot_t.get(sl))
            mk_x = dma("sync", sX[sl], xb[sl], src[k * 128:(k + 1) * 128, :])
            wait("vector", sX[sl], mk_x)
            mkv[k] = xslot_v[sl] = cp("vector", xbf[sl], xb[sl], inc=sV)
            wait("scalar", sV, mkv[k])
            mk_sq = act(xsq[sl], xbf[sl], AF.Square, inc=sS)
            if own:
                wait("tensor", sX[sl], mk_x)
                for q in range(4):
                    mm(ps[q][0:2, :], sw_sb[:, 2 * k:2 * k + 2],
                       xb[sl][:, q * 512:(q + 1) * 512], k == 0, k == NK - 1)
            else:
                wait("tensor", sV, mkv[k])
                for q in range(4):
                    mm(ps[q][0:1, :], ones_bf[:, :],
                       xbf[sl][:, q * 512:(q + 1) * 512], k == 0, k == NK - 1)
            wait("tensor", sS, mk_sq)
            for q in range(4):
                mkt[k] = mm(ps[4 + q][0:1, :], ones_bf[:, :],
                            xsq[sl][:, q * 512:(q + 1) * 512], k == 0, k == NK - 1,
                            inc=sT if q == 3 else None)
            xslot_t[sl] = mkt[k]
        return mkt[NK - 1]

    def bcast(row_ap, dst, qn=4, u8=False):
        n = qn * 512
        scr = (bscr8 if u8 else bscr).ap()[0:1, 0:n]
        wait("sync", sBr, sBr.v)
        mk1 = dma("sync", sBr, scr, row_ap[:, 0:n])
        wait("sync", sBr, mk1)
        return dma("sync", sBr, dst[:, 0:n], scr.partition_broadcast(128))

    pmark("P0stats_own")
    wait("tensor", sSm, mk_small)
    mk_os = stats_pass(xT.ap(), True)
    wait("scalar", sSm, mk_small)
    act(sm(C_S1), sm(C_NU), AF.Abs)
    act(sm(C_S1), sm(C_S1), AF.Exp, scale=-1.0)
    act(sm(C_S1), sm(C_S1), AF.Ln, bias=1.0)
    mk = op("scalar", lambda E: E.copy(sm(C_S2), sm(C_NU)), inc=sS)
    wait("vector", sS, mk)
    wait("vector", sV, mk_ones)
    tsc("vector", sm(C_S2), sm(C_S2), 0.0, AL.max)
    mk_v = tt("vector", sm(C_MAG), sm(C_S2), sm(C_S1), AL.add, inc=sV)
    wait("scalar", sV, mk_v)
    act(sm(C_MAG), sm(C_MAG), AF.Exp)
    mk_mag = act(sm(C_MAG), sm(C_MAG), AF.Exp, scale=-1.0, inc=sS)
    wait("vector", sS, mk_mag)
    tt("vector", sm(C_GAM), sm(C_MAG), sm(C_MAG), AL.mult)
    ts2("vector", sm(C_GAM), sm(C_GAM), -1.0, 1.0, AL.mult, AL.add)
    mk_v = tsc("vector", sm(C_GAM), sm(C_GAM), 0.0, AL.max, inc=sV)
    wait("scalar", sV, mk_v)
    act(sm(C_GAM), sm(C_GAM), AF.Sqrt)
    mk_gam = act(sm(C_GAM), sm(C_GAM), AF.Sqrt, inc=sS)
    act(sm(C_TH), sm(C_TH), AF.Exp)
    mk_th = op("scalar", lambda E: E.copy(sm(C_S1), sm(C_TH)), inc=sS)
    wait("vector", sS, mk_th)
    for _ in range(8):
        tsc("vector", sm(C_S2), sm(C_S1), PI, AL.is_gt)
        mk_v = stt("vector", sm(C_S1), sm(C_S2), -TWOPI, sm(C_S1),
                   AL.mult, AL.add, inc=sV)
    wait("scalar", sV, mk_v)
    act(sm(C_SIN), sm(C_S1), AF.Sin)
    mk_sh = act(sm(C_S2), sm(C_S1), AF.Sin, scale=0.5, inc=sS)
    wait("vector", sS, mk_sh)
    tt("vector", sm(C_COS), sm(C_S2), sm(C_S2), AL.mult)
    mk_cos = ts2("vector", sm(C_COS), sm(C_COS), -2.0, 1.0, AL.mult, AL.add,
                 inc=sV)

    Rre = [arF(16 * g, T) for g in range(NG)]
    Rim = [arF(8 + 16 * g, T) for g in range(NG)]
    pAT = arF(112, T)

    wait("vector", sV, mk_cos)
    for g in range(NG):
        op("vector", lambda E, g=g: E.memset(Rre[g][:, 0:1], 1.0))
        op("vector", lambda E, g=g: E.memset(Rim[g][:, 0:1], 0.0))
        cp("vector", Rre[g][:, 1:2], small[:, C_COS + g:C_COS + g + 1])
        tsc("vector", Rim[g][:, 1:2], small[:, C_SIN + g:C_SIN + g + 1],
            -1.0, AL.mult)
        n = 2
        while n < T:
            pre, pim, t1 = smc(C_S1), smc(C_S2), smc(C_T1)
            a, b = Rre[g][:, n - 1:n], Rim[g][:, n - 1:n]
            c, d = Rre[g][:, 1:2], Rim[g][:, 1:2]
            tt("vector", pre, a, c, AL.mult)
            tt("vector", t1, b, d, AL.mult)
            tt("vector", pre, pre, t1, AL.subtract)
            tt("vector", pim, a, d, AL.mult)
            tt("vector", t1, b, c, AL.mult)
            tt("vector", pim, pim, t1, AL.add)
            w = min(n, T - n)
            sA, sB = pAT[:, 0:w], pAT[:, w:2 * w]
            tsc("vector", sA, Rre[g][:, 0:w], pre, AL.mult)
            tsc("vector", sB, Rim[g][:, 0:w], pim, AL.mult)
            tt("vector", Rre[g][:, n:n + w], sA, sB, AL.subtract)
            tsc("vector", sA, Rre[g][:, 0:w], pim, AL.mult)
            tsc("vector", sB, Rim[g][:, 0:w], pre, AL.mult)
            tt("vector", Rim[g][:, n:n + w], sA, sB, AL.add)
            n *= 2
    wait("vector", sS, mk_gam)
    for g in range(NG):
        gam = smc(C_GAM + g)
        tsc("vector", R_bf[:, (2 * g) * T:(2 * g + 1) * T], Rre[g], gam, AL.mult)
        tsc("vector", R_bf[:, (2 * g + 1) * T:(2 * g + 2) * T], Rim[g], gam,
            AL.mult)
    for g in range(NG):
        a, b = Rre[g][:, T - 1:T], Rim[g][:, T - 1:T]
        c, d = Rre[g][:, 1:2], Rim[g][:, 1:2]
        pre, pim, t1 = smc(28 + 2 * g), smc(28 + 2 * g + 1), smc(C_T1)
        tt("vector", pre, a, c, AL.mult)
        tt("vector", t1, b, d, AL.mult)
        tt("vector", pre, pre, t1, AL.subtract)
        tt("vector", pim, a, d, AL.mult)
        tt("vector", t1, b, c, AL.mult)
        tt("vector", pim, pim, t1, AL.add)
        tsc("vector", pim, pim, -1.0, AL.mult)
    milestone("vector")

    wait("vector", sT, mk_os)
    for q in range(4):
        cp("vector", row2[0:2, q * 512:(q + 1) * 512], ps[q][0:2, :])
    for q in range(4):
        cp("vector", row_sx2[0:1, q * 512:(q + 1) * 512], ps[4 + q][0:1, :])
    mk_ev = milestone("vector")
    wait("sync", sV, mk_ev)
    mk_s1 = dma("sync", sH[0], row_S1[0:1, :], row2[1:2, :])

    fence(("tensor",))
    mk_ns = stats_pass(xNB.ap(), False)
    rowm = row2[0:1, :]
    tsc("vector", rowm, rowm, INV_D, AL.mult)
    mk_m = milestone("vector")
    wait("sync", sV, mk_m)
    bcast(rowm, m_bc)
    wait("scalar", sV, mk_m)
    mk_m2 = act(row_tmp[0:1, :], rowm, AF.Square, inc=sS)
    wait("vector", sS, mk_m2)
    stt("vector", row_sx2[0:1, :], row_sx2[0:1, :], INV_D, row_tmp[0:1, :],
        AL.mult, AL.subtract)
    tsc("vector", row_tmp[0:1, :], row_sx2[0:1, :], 1e-5, AL.add)
    mk_v = milestone("vector")
    wait("scalar", sV, mk_v)
    mk_sq = act(row_tmp[0:1, :], row_tmp[0:1, :], AF.Sqrt, inc=sS)
    wait("vector", sS, mk_sq)
    op("vector", lambda E: E.reciprocal(row_tmp[0:1, :], row_tmp[0:1, :]))
    mk_r = milestone("vector")
    wait("sync", sV, mk_r)
    bcast(row_tmp[0:1, :], r_bc)
    wait("vector", sBr, sBr.v)
    tsc("vector", rowm, row_sx2[0:1, :], 1e-5, AL.add)
    op("vector", lambda E: E.reciprocal(rowm, rowm))
    tt("vector", rowm, row_sx2[0:1, :], rowm, AL.mult)
    tsc("vector", rowm, rowm, 1e-6, AL.add)
    mk_v = milestone("vector")
    wait("scalar", sV, mk_v)
    mk_sq = act(rowm, rowm, AF.Sqrt, inc=sS)
    wait("vector", sS, mk_sq)
    op("vector", lambda E: E.reciprocal(rowm, rowm))
    tt("vector", rowm, rowm, row_tmp[0:1, :], AL.mult)
    mk_rr = milestone("vector")
    wait("sync", sV, mk_rr)
    bcast(rowm, rr1_bc)
    wait("vector", sT, mk_ns)
    wait("vector", sBr, sBr.v)
    for q in range(4):
        cp("vector", row2[0:1, q * 512:(q + 1) * 512], ps[q][0:1, :])
    for q in range(4):
        cp("vector", row_sx2[0:1, q * 512:(q + 1) * 512], ps[4 + q][0:1, :])
    mk_nbev = milestone("vector")
    onescol = arF(140, 1)
    op("vector", lambda E: E.memset(onescol, 1.0))
    mk_oc = milestone("vector")
    wait("tensor", sV, mk_oc)
    wait("tensor", sV, mk_nbev)
    mk_S2 = None
    for k in range(NK):
        mk_S2 = mm(ps[5][0:1, 0:1], sw_sb[:, 2 * k + 1:2 * k + 2], onescol,
                   k == 0, k == NK - 1, inc=sT if k == NK - 1 else None)
    wait("vector", sT, mk_S2)
    cp("vector", small[0:1, C_T1:C_T1 + 1], ps[5][0:1, 0:1])
    wait("vector", sH[0], mk_s1)
    wait("vector", sBr, sBr.v)
    tsc("vector", row_tmp[0:1, :], m_bc[0:1, :], small[0:1, C_T1:C_T1 + 1],
        AL.mult)
    tt("vector", row_tmp[0:1, :], row_S1[0:1, :], row_tmp[0:1, :], AL.subtract)
    tt("vector", row_tmp[0:1, :], row_tmp[0:1, :], r_bc[0:1, :], AL.mult)
    tsc("vector", row_tmp[0:1, :], row_tmp[0:1, :], br_sb[0:1, 0:1], AL.add)
    tsc("vector", row_S1[0:1, :].bitcast(mybir.dt.uint8)[0:1, 0:T],
        row_tmp[0:1, :], 0.0, AL.is_gt)
    mk_msk = milestone("vector")
    wait("sync", sV, mk_msk)
    bcast(row_S1[0:1, :].bitcast(mybir.dt.uint8), mask_bc, u8=True)
    wait("vector", sBr, sBr.v)
    tsc("vector", rowm, rowm, INV_D, AL.mult)
    mk_m = milestone("vector")
    wait("scalar", sV, mk_m)
    mk_m2 = act(row_tmp[0:1, :], rowm, AF.Square, inc=sS)
    wait("vector", sS, mk_m2)
    stt("vector", row_sx2[0:1, :], row_sx2[0:1, :], INV_D, row_tmp[0:1, :],
        AL.mult, AL.subtract)
    tsc("vector", row_sx2[0:1, :], row_sx2[0:1, :], 1e-5, AL.add)
    mk_v = milestone("vector")
    wait("scalar", sV, mk_v)
    mk_sq = act(row_sx2[0:1, :], row_sx2[0:1, :], AF.Sqrt, inc=sS)
    wait("vector", sS, mk_sq)
    op("vector", lambda E: E.reciprocal(row_sx2[0:1, :], row_sx2[0:1, :]))
    mk_rnb = milestone("vector")
    mnb_bc = arF(120, T)
    rnb_bc = arF(128, T)
    wait("sync", sV, mk_m)
    bcast(rowm, mnb_bc)
    wait("sync", sV, mk_rnb)
    bcast(row_sx2[0:1, :], rnb_bc)


    pmark("P0stats_nb")
    wait("tensor", sBc, mk_bcat)
    fence(("tensor",))
    mk_bs = None
    for pl in range(4):
        for k in range(NK):
            mk_bs = mm(ps[3][:, pl:pl + 1],
                       bcat_sb[:, k * 512 + pl * 128:k * 512 + (pl + 1) * 128],
                       negones_bf[:, :], k == 0, k == NK - 1,
                       inc=sT if k == NK - 1 else None)
    wait("vector", sT, mk_bs)
    cp("vector", smc(C_BS + 0), ps[3][:, 0:1])
    cp("vector", smc(C_BS + 1), ps[3][:, 2:3])
    cp("vector", smc(C_BS + 2), ps[3][:, 1:2])
    mk_bsum = cp("vector", smc(C_BS + 3), ps[3][:, 3:4], inc=sV)

    upl = [arF(8 * i, T) for i in range(4)]
    xbh = [arF(112 + 4 * i, TS) for i in range(2)]
    xbf_s = [arB(136 + 2 * i, TS) for i in range(2)]
    rotA = arF(76, T)
    rotB = arF(128, T)

    def bu_pass(src, mbc, rbc, mk_ps_free):
        wait("tensor", sV, mk_ps_free)
        wait("vector", sBr, sBr.v)
        mk_free = None
        for qp in range(2):
            c0 = qp * TS
            mkt = {}
            for k in range(NK):
                sl = k % 2
                wait("sync", sV, xslot_v.get(sl))
                mk_x = dma("sync", sX[sl], xbh[sl],
                           src[k * 128:(k + 1) * 128, c0:c0 + TS])
                wait("vector", sX[sl], mk_x)
                wait("vector", sT, xslot_t.get(sl))
                xslot_v[sl] = cp("vector", xbf_s[sl], xbh[sl], inc=sV)
                wait("tensor", sV, xslot_v[sl])
                for pl in range(4):
                    for q in range(2):
                        mkt[k] = mm(ps[pl * 2 + q][:, :],
                                    bcat_sb[:, k * 512 + pl * 128:
                                            k * 512 + (pl + 1) * 128],
                                    xbf_s[sl][:, q * 512:(q + 1) * 512],
                                    k == 0, k == NK - 1,
                                    inc=sT if (pl == 3 and q == 1) else None)
                xslot_t[sl] = mkt[k]
            wait("vector", sT, mkt[NK - 1])
            perm = [0, 2, 1, 3]
            for di in range(4):
                pl = perm[di]
                for q in range(2):
                    cc = c0 + q * 512
                    cp("vector", upl[di][:, cc:cc + 512], ps[pl * 2 + q][:, :])
            mk_free = milestone("vector")
            wait("tensor", sV, mk_free)
            for di in range(4):
                stt("vector", upl[di][:, c0:c0 + TS], mbc[:, c0:c0 + TS],
                    smc(C_BS + di), upl[di][:, c0:c0 + TS], AL.mult, AL.add)
                tt("vector", upl[di][:, c0:c0 + TS], upl[di][:, c0:c0 + TS],
                   rbc[:, c0:c0 + TS], AL.mult)
        mk_v = milestone("vector")
        return mk_free, mk_v

    def rotate_scan(init_cols):
        for g in range(NG):
            Rr = R_bf[:, (2 * g) * T:(2 * g + 1) * T]
            Ri = R_bf[:, (2 * g + 1) * T:(2 * g + 2) * T]
            ur, ui = upl[2 * g], upl[2 * g + 1]
            tt("vector", rotA, ur, Rr, AL.mult)
            tt("vector", rotB, ui, Ri, AL.mult)
            tt("vector", rotA, rotA, rotB, AL.subtract)
            tt("vector", rotB, ui, Rr, AL.mult)
            tt("vector", ui, ur, Ri, AL.mult)
            tt("vector", rotB, rotB, ui, AL.add)
            magc = smc(C_MAG + g)
            for ri, dsrc in ((0, rotA), (1, rotB)):
                i = 2 * g + ri
                ini = init_cols[i] if init_cols else 0.0
                op("vector",
                   lambda E, o=upl[i], d1=dsrc, mc=magc, iv=ini:
                   E.tensor_tensor_scan(o, mc.broadcast_to([128, T]), d1,
                                        iv, AL.mult, AL.add),
                   inc=sV)

    pmark("P0bu_nb")
    wait("vector", sV, mk_bsum)
    mk_ps0 = milestone("vector")
    nb_free, nb_fv = bu_pass(xNB.ap(), mnb_bc, rnb_bc, mk_ps0)
    rotate_scan(None)
    for g in range(NG):
        pre, pim = smc(28 + 2 * g), smc(28 + 2 * g + 1)
        t1 = smc(C_T1)
        ge_r = upl[2 * g][:, T - 1:T]
        ge_i = upl[2 * g + 1][:, T - 1:T]
        gi_r, gi_i = smc(C_GI + 2 * g), smc(C_GI + 2 * g + 1)
        tsc("vector", gi_r, ge_r, pre, AL.mult)
        tsc("vector", t1, ge_i, pim, AL.mult)
        tt("vector", gi_r, gi_r, t1, AL.subtract)
        tsc("vector", gi_i, ge_r, pim, AL.mult)
        tsc("vector", t1, ge_i, pre, AL.mult)
        tt("vector", gi_i, gi_i, t1, AL.add)
        tsc("vector", gi_r, gi_r, cmask_sb[:, 0:1], AL.mult)
        tsc("vector", gi_i, gi_i, cmask_sb[:, 0:1], AL.mult)
    milestone("vector")

    pmark("P0bu_own")
    ow_free, ow_fv = bu_pass(xT.ap(), m_bc, r_bc, nb_free)

    def lru_tail_own():
        rotate_scan([smc(C_GI + i) for i in range(4)])
        for i in range(4):
            cp("vector", g_bf[:, i * T:(i + 1) * T], upl[i])
        milestone("vector")

    h_pl = [arF(4 * k, TS) if k < 8 else arF(64 + 4 * (k - 8), TS)
            for k in range(NK)]
    ff_pl = [arB(2 * j, TS) for j in range(FFC)]
    h1_pl = [arB(88 + 2 * k, TS) for k in range(NK)]
    h2_pl = h1_pl
    v_pl = [arB(32 + 2 * e, TS) for e in range(NK)]
    C_sb = arB(88, 4 * D)
    Hbf = [arB(104 + 2 * i, TS) for i in range(4)]
    t6a = arF(112, 512)
    t6b_ = [arF(114, 512), arF(116, 512)]
    t6g = arF(118, 512)
    xb3 = [arF(64, TS), arF(68, TS)]
    t3 = arF(72, TS)
    xb4 = arF(132, TS)
    hsq4 = [arB(140, TS), arB(138, TS)]
    t5 = [arF(136, 512), arF(138, 512)]
    rms2_bc = arF(128, TS)
    p6x = [arF(132, 512), arF(134, 512)]
    p6h = [arF(136, 512), arF(138, 512)]
    wring = [arB(120, NK * 128), arB(124, NK * 128)]
    upring = [arB(128, NK * 128), arB(132, NK * 128)]
    dring = [arB(120, 22 * 128), arB(125.5, 22 * 128)]

    def phase3(s):
        c0 = s * TS
        fence()
        mk_h1 = {}
        mk_t3u = {}
        for k in range(NK):
            sl = k % 2
            if k >= 2:
                wait("sync", sV, mk_t3u[k - 2])
            mk_x = dma("sync", sX[sl], xb3[sl], xT[k * 128:(k + 1) * 128, c0:c0 + TS])
            wait("vector", sX[sl], mk_x)
            if k >= 1:
                wait("vector", sS, mk_h1[k - 1])
            mk_t3u[k] = tt("vector", t3, xb3[sl], m_bc[:, c0:c0 + TS],
                           AL.subtract, inc=sV)
            mk_t = tt("vector", t3, t3, rr1_bc[:, c0:c0 + TS], AL.mult, inc=sV)
            wait("scalar", sV, mk_t)
            mk_h1[k] = act(h1_pl[k], t3, AF.Identity,
                           scale=vp[:, 16 + k:16 + k + 1], inc=sS)
        mk_w = {}
        mk_grp = {}
        mk_ev = {}
        for e in range(NK):
            slw = e % 2
            if e >= 2:
                wait("gpsimd", sT, mk_grp[e - 2])
            mk_w[e] = dma("gpsimd", sWv[slw],
                          wring[slw].rearrange("p (k c) -> p k c", k=NK),
                          wv_p[e].rearrange("k p c -> p k c"))
            wait("tensor", sWv[slw], mk_w[e])
            for q in range(2):
                idx = 2 * e + q
                bank = idx % 4
                if idx >= 4:
                    wait("tensor", sS, mk_ev[idx - 4])
                for k in range(NK):
                    if e == 0 and q == 0:
                        wait("tensor", sS, mk_h1[k])
                    mk_grp[e] = mm(ps[bank][:, :],
                                   wring[slw][:, k * 128:(k + 1) * 128],
                                   h1_pl[k][:, q * 512:(q + 1) * 512],
                                   k == 0, k == NK - 1,
                                   inc=sT if k == NK - 1 else None)
                wait("scalar", sT, mk_grp[e])
                mk_ev[idx] = act(v_pl[e][:, q * 512:(q + 1) * 512], ps[bank][:, :],
                                 AF.Identity, bias=vp[:, 48 + e:48 + e + 1],
                                 inc=sS)
        return mk_ev[2 * NK - 1]

    def phase4(s, mk_v_last):
        c0 = s * TS
        fence()
        wait("tensor", sS, mk_v_last)
        mk_w = {}
        mk_grp = {}
        mk_hq = {}
        mk_stq = {}
        mk_sq = {}
        mk_xc = {}
        for d in range(NK):
            slw = d % 2
            if d >= 2:
                wait("gpsimd", sT, mk_grp[d - 2])
            mk_w[d] = dma("gpsimd", sWv[slw],
                          wring[slw].rearrange("p (k c) -> p k c", k=NK),
                          wo_p[d].rearrange("k p c -> p k c"))
            if d >= 1:
                wait("sync", sV, mk_xc[d - 1])
            mk_x = dma("sync", sX[0], xb4, xT[d * 128:(d + 1) * 128, c0:c0 + TS])
            wait("tensor", sWv[slw], mk_w[d])
            for q in range(2):
                idx = 2 * d + q
                bank = idx % 4
                if idx >= 4:
                    wait("tensor", sV, mk_hq[idx - 4])
                for e in range(NK):
                    mk_grp[d] = mm(ps[bank][:, :],
                                   wring[slw][:, e * 128:(e + 1) * 128],
                                   v_pl[e][:, q * 512:(q + 1) * 512],
                                   e == 0, e == NK - 1,
                                   inc=sT if e == NK - 1 else None)
            wait("vector", sX[0], mk_x)
            mk_xc[d] = tt("vector", h_pl[d], xb4, m_bc[:, c0:c0 + TS],
                          AL.subtract, inc=sV)
            tt("vector", h_pl[d], h_pl[d], r_bc[:, c0:c0 + TS], AL.mult)
            wait("vector", sT, mk_grp[d])
            for q in range(2):
                bank = (2 * d + q) % 4
                mk_hq[2 * d + q] = tt("vector", h_pl[d][:, q * 512:(q + 1) * 512],
                                      h_pl[d][:, q * 512:(q + 1) * 512],
                                      ps[bank][:, :], AL.add, inc=sV)
            if d >= 2:
                dd = d - 2
                wait("tensor", sS, mk_sq[dd])
                for q in range(2):
                    mk_stq[dd] = mm(ps[4 + q][0:1, :], ones_bf[:, :],
                                    hsq4[dd % 2][:, q * 512:(q + 1) * 512],
                                    dd == 0, dd == NK - 1,
                                    inc=sT if q == 1 else None)
            wait("scalar", sV, mk_hq[2 * d + 1])
            if d >= 2:
                wait("scalar", sT, mk_stq[d - 2])
            mk_sq[d] = act(hsq4[d % 2], h_pl[d], AF.Square, inc=sS)
            if d == NK - 1:
                for dd in (d - 1, d):
                    wait("tensor", sS, mk_sq[dd])
                    for q in range(2):
                        mk_stq[dd] = mm(ps[4 + q][0:1, :], ones_bf[:, :],
                                        hsq4[dd % 2][:, q * 512:(q + 1) * 512],
                                        dd == 0, dd == NK - 1,
                                        inc=sT if q == 1 else None)
        wait("vector", sT, mk_stq[NK - 1])
        cp("vector", rms2_bc[0:1, 0:512], ps[4][0:1, :])
        cp("vector", rms2_bc[0:1, 512:1024], ps[5][0:1, :])
        ts2("vector", rms2_bc[0:1, :], rms2_bc[0:1, :], INV_D, 1e-6,
            AL.mult, AL.add)
        mk_v = milestone("vector")
        wait("scalar", sV, mk_v)
        mk_sq = act(rms2_bc[0:1, :], rms2_bc[0:1, :], AF.Sqrt, inc=sS)
        wait("vector", sS, mk_sq)
        op("vector", lambda E: E.reciprocal(rms2_bc[0:1, :], rms2_bc[0:1, :]))
        mk_row = milestone("vector")
        wait("sync", sV, mk_row)
        mk_r2bc = bcast(rms2_bc[0:1, :], rms2_bc, qn=2)
        wait("vector", sBr, mk_r2bc)
        mk_psfree4 = milestone("vector")
        h2tmp = [xb4, arF(136, TS)]
        mk_h2 = {}
        mk_tt = {}
        for i, d in enumerate(range(NK - 1, -1, -1)):
            sl = i % 2
            if i >= 2:
                wait("vector", sS, mk_h2[d + 2])
            if d == 3:
                wait("vector", sO, sO.v)
            mk_tt[d] = tt("vector", h2tmp[sl], h_pl[d], rms2_bc[:, :], AL.mult,
                          inc=sV)
            wait("scalar", sV, mk_tt[d])
            mk_h2[d] = act(h2_pl[d], h2tmp[sl], AF.Identity,
                           scale=vp[:, 32 + d:32 + d + 1], inc=sS)
            wait("sync", sV, mk_tt[d])
            dma("sync", sO, hsp[d * 128:(d + 1) * 128, c0:c0 + TS], h_pl[d])
        return mk_h2, sO.v, mk_psfree4, mk_tt[0]

    def phase5(s, mk_h2, mk_psfree4, mk_h2tt):
        fence(skip=(("tensor", "scalar"), ("tensor", "vector"),
                    ("gpsimd", "scalar"), ("gpsimd", "vector")))
        wait("tensor", sV, mk_psfree4)
        mk_wu = {}
        mk_grp = {}
        mk_ff = {}
        mk_si = {}
        for j in range(FFC):
            slw = j % 2
            if j >= 2:
                wait("gpsimd", sT, mk_grp[j - 2])
            if j < 2:
                wait("gpsimd", sV, mk_h2tt)
                wait("gpsimd", sS, mk_h2[0])
            mk_wg = dma("gpsimd", sWv[slw],
                        wring[slw].rearrange("p (k c) -> p k c", k=NK),
                        gate_p[j].rearrange("k p c -> p k c"))
            mk_wu[j] = dma("gpsimd", sUp[slw],
                           upring[slw].rearrange("p (k c) -> p k c", k=NK),
                           up_p[j].rearrange("k p c -> p k c"))
            wait("tensor", sWv[slw], mk_wg)
            wait("tensor", sUp[slw], mk_wu[j])
            bank0 = (j % 2) * 4
            for q in range(2):
                if j >= 2:
                    wait("tensor", sV, mk_ff[(j - 2, q)])
                    wait("tensor", sS, mk_si[(j - 2, q)])
                for i, k in enumerate(range(NK - 1, -1, -1)):
                    if j == 0 and q == 0:
                        wait("tensor", sS, mk_h2[k])
                    mm(ps[bank0 + q][:, :], wring[slw][:, k * 128:(k + 1) * 128],
                       h2_pl[k][:, q * 512:(q + 1) * 512], i == 0, i == NK - 1)
                for i, k in enumerate(range(NK - 1, -1, -1)):
                    mk_grp[j] = mm(ps[bank0 + 2 + q][:, :],
                                   upring[slw][:, k * 128:(k + 1) * 128],
                                   h2_pl[k][:, q * 512:(q + 1) * 512],
                                   i == 0, i == NK - 1,
                                   inc=sT if i == NK - 1 else None)
                wait("scalar", sT, mk_grp[j])
                if j >= 1:
                    wait("scalar", sV, mk_ff.get((j - 1, q), 0) or 0)
                mk_si[(j, q)] = act(t5[q], ps[bank0 + q][:, :], AF.Silu, inc=sS)
                wait("vector", sS, mk_si[(j, q)])
                mk_ff[(j, q)] = tt("vector", ff_pl[j][:, q * 512:(q + 1) * 512],
                                   t5[q], ps[bank0 + 2 + q][:, :], AL.mult,
                                   inc=sV)
        return mk_ff[(FFC - 1, 1)]

    def phase6(s, mk_ff_last, spill_mark):
        c0 = s * TS
        fence()
        mk_C = dma("gpsimd", sBc, C_sb.rearrange("p (q c) -> p q c", q=4),
                   Ccat.ap().rearrange("q p c -> p q c"))
        scr = arF(112, TS)
        for g in range(NG):
            gr = g_bf[:, (2 * g) * T + c0:(2 * g) * T + c0 + TS]
            gi = g_bf[:, (2 * g + 1) * T + c0:(2 * g + 1) * T + c0 + TS]
            rr = R_bf[:, (2 * g) * T + c0:(2 * g) * T + c0 + TS]
            ri = R_bf[:, (2 * g + 1) * T + c0:(2 * g + 1) * T + c0 + TS]
            tt("vector", scr, gr, rr, AL.mult)
            tt("vector", Hbf[2 * g], gi, ri, AL.mult)
            tt("vector", Hbf[2 * g], scr, Hbf[2 * g], AL.add)
            tt("vector", scr, gi, rr, AL.mult)
            tt("vector", Hbf[2 * g + 1], gr, ri, AL.mult)
            tt("vector", Hbf[2 * g + 1], Hbf[2 * g + 1], scr, AL.subtract)
        mk_Hc = milestone("vector")

        wait("sync", sO, spill_mark)
        wait("tensor", sV, mk_ff_last)
        mk_grp = {}
        mk_half = {}
        mk_ep = {}
        mk_g = {}
        mk_gsil = {}
        for d in range(NK):
            if d >= 1:
                wait("gpsimd", sT, mk_half[(d - 1, 1, 0)])
            mk_w0 = dma("gpsimd", sDn[0],
                        dring[0].rearrange("p (j c) -> p j c", j=22),
                        down_p[d, 0:22].rearrange("j p c -> p j c"))
            if d >= 1:
                wait("gpsimd", sT, mk_half[(d - 1, 1, 1)])
            mk_w1 = dma("gpsimd", sDn[1],
                        dring[1].rearrange("p (j c) -> p j c", j=22),
                        down_p[d, 22:44].rearrange("j p c -> p j c"))
            if d >= 1:
                wait("sync", sV, mk_ep[2 * d - 1])
            mk_h = {}
            mk_x = {}
            for q in range(2):
                mk_h[q] = dma("sync", sH[q], p6h[q],
                              hsp[d * 128:(d + 1) * 128,
                                  c0 + q * 512:c0 + (q + 1) * 512])
                mk_x[q] = dma("sync", sX[q], p6x[q],
                              xT[d * 128:(d + 1) * 128,
                                 c0 + q * 512:c0 + (q + 1) * 512])
            for q in range(2):
                idx = 2 * d + q
                bankd = idx % 4
                banky = 4 + idx % 4
                if idx >= 4:
                    wait("tensor", sV, mk_ep[idx - 4])
                if q == 0:
                    wait("tensor", sDn[0], mk_w0)
                for j in range(22):
                    mk_half[(d, q, 0)] = mm(
                        ps[bankd][:, :], dring[0][:, j * 128:(j + 1) * 128],
                        ff_pl[j][:, q * 512:(q + 1) * 512], j == 0, False,
                        inc=sT if j == 21 else None)
                if q == 0:
                    wait("tensor", sDn[1], mk_w1)
                for j in range(22):
                    mk_half[(d, q, 1)] = mm(
                        ps[bankd][:, :], dring[1][:, j * 128:(j + 1) * 128],
                        ff_pl[22 + j][:, q * 512:(q + 1) * 512], False, j == 21,
                        inc=sT if j == 21 else None)
                if d == 0 and q == 0:
                    wait("tensor", sV, mk_Hc)
                    wait("tensor", sBc, mk_C)
                for pi in range(4):
                    g = pi % 2
                    ri = pi // 2
                    mk_grp[idx] = mm(
                        ps[banky][:, :],
                        C_sb[:, pi * D + d * 128:pi * D + (d + 1) * 128],
                        Hbf[2 * g + ri][:, q * 512:(q + 1) * 512],
                        pi == 0, pi == 3, inc=sT if pi == 3 else None)
                cc = c0 + q * 512
                tb = t6b_[idx % 2]
                wait("vector", sT, mk_grp[idx])
                wait("vector", sX[q], mk_x[q])
                wait("vector", sH[q], mk_h[q])
                wait("vector", sOut[idx % 2], sOut[idx % 2].v)
                tt("vector", t6a, p6x[q], m_bc[:, cc:cc + 512], AL.subtract)
                tt("vector", t6a, t6a, r_bc[:, cc:cc + 512], AL.mult)
                tt("vector", tb, p6h[q], ps[bankd][:, :], AL.add)
                if idx >= 1:
                    wait("vector", sS, mk_gsil[idx - 1])
                mk_sel = sel(t6g, mask_bc[:, cc:cc + 512], tb, t6a, inc=sV)
                wait("scalar", sV, mk_sel)
                mk_gsil[idx] = act(t6g, t6g, AF.Silu, inc=sS)
                wait("vector", sS, mk_gsil[idx])
                stt("vector", tb, t6a, vp[:, d:d + 1], ps[banky][:, :],
                    AL.mult, AL.add)
                mk_g[idx] = tt("vector", tb, tb, t6g, AL.mult, inc=sV)
                mk_ep[idx] = tt("vector", tb, tb, p6x[q], AL.add, inc=sV)
                wait("sync", sV, mk_ep[idx])
                dma("sync", sOut[idx % 2],
                    out_d[d * 128:(d + 1) * 128, cc:cc + 512], tb)

    for s in range(NSUB):
        pmark("P3_s%d" % s)
        mk_v_last = phase3(s)
        if s == 0:
            lru_tail_own()
        pmark("P4_s%d" % s)
        mk_h2, spill_mark, mk_pf4, mk_h2tt = phase4(s, mk_v_last)
        pmark("P5_s%d" % s)
        mk_ff_last = phase5(s, mk_h2, mk_pf4, mk_h2tt)
        pmark("P6_s%d" % s)
        phase6(s, mk_ff_last, spill_mark)

    em("sync", lambda E, v=sO.v: E.wait_ge(sO.h, v))
    em("sync", lambda E, v=sOut[0].v: E.wait_ge(sOut[0].h, v))
    em("sync", lambda E, v=sOut[1].v: E.wait_ge(sOut[1].h, v))

    with nc.Block() as block:
        @block.tensor
        def _(E):
            for f in prog["tensor"]:
                f(E)

        @block.vector
        def _(E):
            for f in prog["vector"]:
                f(E)

        @block.scalar
        def _(E):
            for f in prog["scalar"]:
                f(E)

        @block.sync
        def _(E):
            for f in prog["sync"]:
                f(E)

        @block.gpsimd
        def _(E):
            for f in prog["gpsimd"]:
                f(E)

    es.close()
    return nc


_CACHE = {}


def _pack_inputs(inputs):
    f32 = np.float32
    x = np.asarray(inputs["L0_attn_x"], f32)
    wv = np.asarray(inputs["L0_ffn_wv"], f32)
    wo = np.asarray(inputs["L0_ffn_wo"], f32)
    wg = np.asarray(inputs["L0_ffn_w_gate"], f32)
    wu = np.asarray(inputs["L0_ffn_w_up"], f32)
    wd = np.asarray(inputs["L0_ffn_w_down"], f32)
    Bre = np.asarray(inputs["L0_attn_B_re"], f32)
    Bim = np.asarray(inputs["L0_attn_B_im"], f32)
    Cre = np.asarray(inputs["L0_attn_C_re"], f32)
    Cim = np.asarray(inputs["L0_attn_C_im"], f32)

    wv_p = np.ascontiguousarray(wv.reshape(NK, 128, NK, 128).transpose(2, 0, 1, 3))
    wo_p = np.ascontiguousarray(wo.reshape(NK, 128, NK, 128).transpose(2, 0, 1, 3))
    gate_p = np.ascontiguousarray(wg.reshape(NK, 128, FFC, 128).transpose(2, 0, 1, 3))
    up_p = np.ascontiguousarray(wu.reshape(NK, 128, FFC, 128).transpose(2, 0, 1, 3))
    down_p = np.ascontiguousarray(wd.reshape(FFC, 128, NK, 128).transpose(2, 0, 1, 3))
    Bcat = np.ascontiguousarray(
        np.concatenate([Bre.T, Bim.T], axis=1).reshape(NK, 128, 512))
    Ccat = np.ascontiguousarray(np.concatenate(
        [Cre.T.reshape(2, 128, D), Cim.T.reshape(2, 128, D)], axis=0))

    def col16(v):
        return np.ascontiguousarray(np.asarray(v, f32).reshape(NK, 128).T)

    vparams = np.zeros((128, 64), f32)
    vparams[:, 0:16] = col16(inputs["L0_attn_D"])
    vparams[:, 16:32] = col16(inputs["L0_ffn_norm1_w"])
    vparams[:, 32:48] = col16(inputs["L0_ffn_norm2_w"])
    vparams[:, 48:64] = col16(inputs["L0_ffn_bv"])

    stats_w = np.zeros((128, 32), f32)
    wr = col16(inputs["L0_ffn_w_router"])
    for k in range(NK):
        stats_w[:, 2 * k] = 1.0
        stats_w[:, 2 * k + 1] = wr[:, k]

    lru = np.zeros((128, 4), f32)
    lru[:, 0:2] = np.asarray(inputs["L0_attn_nu_log"], f32).reshape(2, 128).T
    lru[:, 2:4] = np.asarray(inputs["L0_attn_theta_log"], f32).reshape(2, 128).T

    br = np.asarray(inputs["L0_ffn_b_router"], f32).reshape(1, 1)

    shared = dict(wv_p=wv_p, wo_p=wo_p, gate_p=gate_p, up_p=up_p, down_p=down_p,
                  Bcat=Bcat, Ccat=Ccat, vparams=vparams, stats_w=stats_w,
                  lru_in=lru, brouter=br)
    in_maps = []
    for c in range(8):
        b, half = c // 2, c % 2
        xs = np.ascontiguousarray(x[b, half * T:(half + 1) * T, :].T)
        xnb = np.ascontiguousarray(x[b, 0:T, :].T) if half == 1 else xs
        cm = np.full((128, 1), float(half), f32)
        m = dict(shared)
        m.update(xT=xs, xNB=xnb, cmask=cm)
        in_maps.append(m)
    return in_maps


def kernel(**inputs):
    in_maps = _pack_inputs(inputs)
    if "nc" not in _CACHE:
        _CACHE["nc"] = build_nc()
    nc = _CACHE["nc"]
    res = run_bass_kernel_spmd(nc, in_maps, core_ids=list(range(8)))
    out = np.empty((4, 4096, D), np.float32)
    for c in range(8):
        b, half = c // 2, c % 2
        out[b, half * T:(half + 1) * T, :] = res.results[c]["out"].T
    return out


# version 10
# speedup vs baseline: 1.0262x; 1.0262x over previous
import numpy as np
import contextlib
import concourse.bass as bass
import concourse.mybir as mybir
from concourse.bass_utils import run_bass_kernel_spmd

F32 = mybir.dt.float32
BF16 = mybir.dt.bfloat16
AL = mybir.AluOpType
AF = mybir.ActivationFunctionType

D = 2048
NK = 16
T = 2048
TS = 1024
NSUB = 2
FFC = 44
NG = 2
INV_D = float(np.float32(1.0 / D))
TWOPI = float(np.float32(2 * np.pi))
PI = float(np.pi)

ENGINES = ("tensor", "vector", "scalar", "sync", "gpsimd")


class SemC:
    def __init__(self, h):
        self.h = h
        self.v = 0


def build_nc():
    nc = bass.Bass("TRN2", target_bir_lowering=False)

    xT = nc.dram_tensor("xT", [D, T], F32, kind="ExternalInput")
    xNB = nc.dram_tensor("xNB", [D, T], F32, kind="ExternalInput")
    wv_p = nc.dram_tensor("wv_p", [NK, NK, 128, 128], F32, kind="ExternalInput")
    wo_p = nc.dram_tensor("wo_p", [NK, NK, 128, 128], F32, kind="ExternalInput")
    gate_p = nc.dram_tensor("gate_p", [FFC, NK, 128, 128], F32, kind="ExternalInput")
    up_p = nc.dram_tensor("up_p", [FFC, NK, 128, 128], F32, kind="ExternalInput")
    down_p = nc.dram_tensor("down_p", [NK, FFC, 128, 128], F32, kind="ExternalInput")
    Bcat = nc.dram_tensor("Bcat", [NK, 128, 512], F32, kind="ExternalInput")
    Ccat = nc.dram_tensor("Ccat", [4, 128, D], F32, kind="ExternalInput")
    vparams = nc.dram_tensor("vparams", [128, 64], F32, kind="ExternalInput")
    stats_w = nc.dram_tensor("stats_w", [128, 32], F32, kind="ExternalInput")
    lru_in = nc.dram_tensor("lru_in", [128, 4], F32, kind="ExternalInput")
    brouter = nc.dram_tensor("brouter", [1, 1], F32, kind="ExternalInput")
    cmask = nc.dram_tensor("cmask", [128, 1], F32, kind="ExternalInput")
    out_d = nc.dram_tensor("out", [D, T], F32, kind="ExternalOutput")
    hsp = nc.dram_tensor("hspill", [D, T], F32)
    bscr = nc.dram_tensor("bscr", [1, T], F32)
    bscr8 = nc.dram_tensor("bscr8", [1, T], mybir.dt.uint8)

    m_bc = nc.alloc_sbuf_tensor("m_bc", [128, T], F32)
    r_bc = nc.alloc_sbuf_tensor("r_bc", [128, T], F32)
    rr1_bc = nc.alloc_sbuf_tensor("rr1_bc", [128, T], F32)
    mask_bc = nc.alloc_sbuf_tensor("mask_bc", [128, T], mybir.dt.uint8)
    g_bf = nc.alloc_sbuf_tensor("g_bf", [128, 4 * T], BF16)
    R_bf = nc.alloc_sbuf_tensor("R_bf", [128, 4 * T], BF16)
    vp = nc.alloc_sbuf_tensor("vp", [128, 64], F32)
    small = nc.alloc_sbuf_tensor("small", [128, 32], F32)
    ones1 = nc.alloc_sbuf_tensor("ones1", [1, 128], F32)
    ones_bf = nc.alloc_sbuf_tensor("ones_bf", [128, 1], BF16)
    negones_bf = nc.alloc_sbuf_tensor("negones_bf", [128, 1], BF16)
    cmask_sb = nc.alloc_sbuf_tensor("cmask_sb", [128, 1], F32)
    br_sb = nc.alloc_sbuf_tensor("br_sb", [1, 1], F32)
    sw_sb = nc.alloc_sbuf_tensor("sw_sb", [128, 32], F32)

    AR_F32 = 36352
    arena = nc.alloc_sbuf_tensor("arena", [128, AR_F32], F32)
    arena_bf = arena.ap().bitcast(BF16)

    def arF(off_kb, cols):
        o = int(off_kb * 256)
        assert o + cols <= AR_F32, (off_kb, cols)
        return arena[:, o:o + cols]

    def arB(off_kb, cols):
        o = int(off_kb * 512)
        assert o + cols <= 2 * AR_F32, (off_kb, cols)
        return arena_bf[:, o:o + cols]

    ps = [nc.alloc_psum_tensor("ps%d" % i, [128, 512], F32) for i in range(8)]

    es = contextlib.ExitStack()
    def mksem(name):
        return SemC(es.enter_context(nc.semaphore(name)))
    sSm = mksem("sSm")
    sBc = mksem("sBc")
    sX = [mksem("sX0"), mksem("sX1"), mksem("sX2"), mksem("sX3")]
    sH = [mksem("sH0"), mksem("sH1")]
    sWv = [mksem("sWv0"), mksem("sWv1")]
    sUp = [mksem("sUp0"), mksem("sUp1")]
    sDn = [mksem("sDn0"), mksem("sDn1")]
    sO = mksem("sO")
    sBr = mksem("sBr")
    sOut = [mksem("sOut0"), mksem("sOut1")]
    sT = mksem("sT")
    sV = mksem("sV")
    sS = mksem("sS")
    sG = mksem("sG")
    ENG_SEM = {"tensor": sT, "vector": sV, "scalar": sS, "gpsimd": sG}
    DMA_SEMS = (sSm, sBc, sX[0], sX[1], sX[2], sX[3], sH[0], sH[1],
                sWv[0], sWv[1], sUp[0], sUp[1], sDn[0], sDn[1], sO, sBr,
                sOut[0], sOut[1])

    prog = {e: [] for e in ENGINES}
    phase_marks = []

    def pmark(name):
        phase_marks.append((name, len(prog["tensor"])))
    build_nc.phase_marks = phase_marks
    lastw = {}

    dirty = {e: False for e in ENGINES}

    def em(e, fn):
        prog[e].append(fn)

    def drain(e):
        if dirty[e]:
            prog[e].append(lambda E: E.drain())
            dirty[e] = False

    def wait(e, s, v=None):
        v = s.v if v is None else v
        if not v or v <= 0:
            return
        key = (e, id(s))
        if lastw.get(key, -1) >= v:
            return
        lastw[key] = v
        em(e, lambda E, h=s.h, vv=v: E.wait_ge(h, vv))

    def op(e, fn, inc=None, n=1, is_dma=False):
        if e in ("vector", "scalar", "gpsimd") and not is_dma:
            drain(e)
            dirty[e] = True
        if inc is None:
            em(e, lambda E, f=fn: f(E))
            return None
        inc.v += n
        mk = inc.v
        em(e, lambda E, f=fn, h=inc.h, nn=n: f(E).then_inc(h, nn))
        return mk

    def dma(queue, sem, out_ap, in_ap):
        return op(queue, lambda E, o=out_ap, i=in_ap: E.dma_start(out=o, in_=i),
                  inc=sem, n=16, is_dma=True)

    def mm(out_ap, lhs_ap, rhs_ap, start, stop, inc=None):
        return op("tensor",
                  lambda E, o=out_ap, l=lhs_ap, r=rhs_ap, a=start, b=stop:
                  E.matmul(o, l, r, start=a, stop=b), inc=inc)

    def tt(e, out_ap, a_ap, b_ap, alu, inc=None):
        return op(e, lambda E, o=out_ap, a=a_ap, b=b_ap, al=alu:
                  E.tensor_tensor(o, a, b, al), inc=inc)

    def tsc(e, out_ap, a_ap, s1, alu, inc=None):
        return op(e, lambda E, o=out_ap, a=a_ap, s=s1, al=alu:
                  E.tensor_single_scalar(o, a, s, al), inc=inc)

    def ts2(e, out_ap, a_ap, s1, s2, op0, op1, inc=None):
        return op(e, lambda E, o=out_ap, a=a_ap, x=s1, y=s2, o0=op0, o1=op1:
                  E.tensor_scalar(o, a, x, y, o0, o1), inc=inc)

    def stt(e, out_ap, a_ap, sc, b_ap, op0, op1, inc=None):
        return op(e, lambda E, o=out_ap, a=a_ap, s=sc, b=b_ap, o0=op0, o1=op1:
                  E.scalar_tensor_tensor(o, a, s, b, o0, o1), inc=inc)

    def act(out_ap, in_ap, func, bias=0.0, scale=1.0, inc=None):
        return op("scalar",
                  lambda E, o=out_ap, i=in_ap, f=func, b=bias, s=scale:
                  E.activation(o, i, f, bias=b, scale=s), inc=inc)

    def cp(e, out_ap, in_ap, inc=None):
        return op(e, lambda E, o=out_ap, i=in_ap: E.tensor_copy(o, i), inc=inc)

    def sel(out_ap, m_ap, a_ap, b_ap, inc=None):
        return op("vector", lambda E, o=out_ap, m=m_ap, a=a_ap, b=b_ap:
                  E.select(o, m, a, b, add_drain=True), inc=inc)

    def milestone(e):
        s = ENG_SEM[e]
        s.v += 1
        mk = s.v
        em(e, lambda E, h=s.h: E.drain().then_inc(h, 1))
        dirty[e] = False
        return mk

    def fence(writers=ENGINES, skip=()):
        marks = {e: milestone(e) for e in ("tensor", "vector", "scalar", "gpsimd")}
        for e in writers:
            for pe, mk in marks.items():
                if pe != e and (e, pe) not in skip:
                    wait(e, ENG_SEM[pe], mk)
            for dsem in DMA_SEMS:
                wait(e, dsem, dsem.v)

    C_NU, C_TH, C_MAG, C_COS, C_SIN = 0, 2, 4, 6, 8
    C_S1, C_S2 = 10, 12
    C_GI = 16
    C_BS = 20
    C_T1 = 24
    C_GAM = 26

    def sm(c, w=2):
        return small[:, c:c + w]

    def smc(c):
        return small[:, c:c + 1]

    pmark("P0a")
    dma("sync", sSm, vp[:, :], vparams.ap())
    dma("sync", sSm, small[:, 0:4], lru_in.ap())
    dma("sync", sSm, sw_sb[:, :], stats_w.ap())
    dma("sync", sSm, br_sb[:, :], brouter.ap())
    mk_small = dma("sync", sSm, cmask_sb[:, :], cmask.ap())
    op("vector", lambda E: E.memset(ones1[:, :], 1.0))
    op("vector", lambda E: E.memset(ones_bf[:, :], 1.0))
    mk_ones = op("vector", lambda E: E.memset(negones_bf[:, :], -1.0), inc=sV)

    bcat_sb = arB(96, NK * 512)
    mk_bcat = dma("gpsimd", sBc, bcat_sb.rearrange("p (k c) -> p k c", k=NK),
                  Bcat.ap().rearrange("k p c -> p k c"))

    xring = [arF(64, T), arF(72, T), arF(112, T), arF(120, T)]
    xbf = [arB(80 + 4 * i, T) for i in range(2)]
    xsq = [arB(88 + 4 * i, T) for i in range(2)]
    ring_v = {}
    ring_t = {}
    bf_t = {}
    xbf_sc = {}
    xbf_tm = {}
    xsq_tm = {}
    row2 = arF(32, T)
    row_S1 = arF(40, T)
    row_sx2 = arF(48, T)
    row_tmp = arF(56, T)

    def stats_pass(src, own):
        mkv = {}
        mkt = {}
        mk_sq = {}
        for k in range(NK):
            sl = k % 4
            sb = k % 2
            wait("sync", sV, ring_v.get(sl))
            wait("sync", sT, ring_t.get(sl))
            mk_x = dma("sync", sX[sl], xring[sl], src[k * 128:(k + 1) * 128, :])
            wait("vector", sX[sl], mk_x)
            wait("vector", sS, xbf_sc.get(sb))
            wait("vector", sT, xbf_tm.get(sb))
            mkv[k] = ring_v[sl] = cp("vector", xbf[sb], xring[sl], inc=sV)
            wait("scalar", sV, mkv[k])
            wait("scalar", sT, xsq_tm.get(sb))
            mk_sq[k] = xbf_sc[sb] = act(xsq[sb], xbf[sb], AF.Square, inc=sS)
            if own:
                wait("tensor", sX[sl], mk_x)
                for q in range(4):
                    ring_t[sl] = mm(ps[q][0:2, :], sw_sb[:, 2 * k:2 * k + 2],
                                    xring[sl][:, q * 512:(q + 1) * 512],
                                    k == 0, k == NK - 1,
                                    inc=sT if q == 3 else None)
            else:
                wait("tensor", sV, mkv[k])
                for q in range(4):
                    mk_nbx = mm(ps[q][0:1, :], ones_bf[:, :],
                                xbf[sb][:, q * 512:(q + 1) * 512],
                                k == 0, k == NK - 1,
                                inc=sT if q == 3 else None)
                xbf_tm[sb] = mk_nbx
            wait("tensor", sS, mk_sq[k])
            for q in range(4):
                mkt[k] = mm(ps[4 + q][0:1, :], ones_bf[:, :],
                            xsq[sb][:, q * 512:(q + 1) * 512], k == 0, k == NK - 1,
                            inc=sT if q == 3 else None)
            xsq_tm[sb] = mkt[k]
        return mkt[NK - 1]

    def bcast(row_ap, dst, qn=4, u8=False):
        n = qn * 512
        scr = (bscr8 if u8 else bscr).ap()[0:1, 0:n]
        wait("sync", sBr, sBr.v)
        mk1 = dma("sync", sBr, scr, row_ap[:, 0:n])
        wait("sync", sBr, mk1)
        return dma("sync", sBr, dst[:, 0:n], scr.partition_broadcast(128))

    pmark("P0stats_own")
    wait("tensor", sSm, mk_small)
    mk_os = stats_pass(xT.ap(), True)
    wait("scalar", sSm, mk_small)
    act(sm(C_S1), sm(C_NU), AF.Abs)
    act(sm(C_S1), sm(C_S1), AF.Exp, scale=-1.0)
    act(sm(C_S1), sm(C_S1), AF.Ln, bias=1.0)
    mk = op("scalar", lambda E: E.copy(sm(C_S2), sm(C_NU)), inc=sS)
    wait("vector", sS, mk)
    wait("vector", sV, mk_ones)
    tsc("vector", sm(C_S2), sm(C_S2), 0.0, AL.max)
    mk_v = tt("vector", sm(C_MAG), sm(C_S2), sm(C_S1), AL.add, inc=sV)
    wait("scalar", sV, mk_v)
    act(sm(C_MAG), sm(C_MAG), AF.Exp)
    mk_mag = act(sm(C_MAG), sm(C_MAG), AF.Exp, scale=-1.0, inc=sS)
    wait("vector", sS, mk_mag)
    tt("vector", sm(C_GAM), sm(C_MAG), sm(C_MAG), AL.mult)
    ts2("vector", sm(C_GAM), sm(C_GAM), -1.0, 1.0, AL.mult, AL.add)
    mk_v = tsc("vector", sm(C_GAM), sm(C_GAM), 0.0, AL.max, inc=sV)
    wait("scalar", sV, mk_v)
    act(sm(C_GAM), sm(C_GAM), AF.Sqrt)
    mk_gam = act(sm(C_GAM), sm(C_GAM), AF.Sqrt, inc=sS)
    act(sm(C_TH), sm(C_TH), AF.Exp)
    mk_th = op("scalar", lambda E: E.copy(sm(C_S1), sm(C_TH)), inc=sS)
    wait("vector", sS, mk_th)
    for _ in range(8):
        tsc("vector", sm(C_S2), sm(C_S1), PI, AL.is_gt)
        mk_v = stt("vector", sm(C_S1), sm(C_S2), -TWOPI, sm(C_S1),
                   AL.mult, AL.add, inc=sV)
    wait("scalar", sV, mk_v)
    act(sm(C_SIN), sm(C_S1), AF.Sin)
    mk_sh = act(sm(C_S2), sm(C_S1), AF.Sin, scale=0.5, inc=sS)
    wait("vector", sS, mk_sh)
    tt("vector", sm(C_COS), sm(C_S2), sm(C_S2), AL.mult)
    mk_cos = ts2("vector", sm(C_COS), sm(C_COS), -2.0, 1.0, AL.mult, AL.add,
                 inc=sV)

    Rre = [arF(16 * g, T) for g in range(NG)]
    Rim = [arF(8 + 16 * g, T) for g in range(NG)]
    pAT = arF(32, T)

    wait("vector", sV, mk_cos)
    for g in range(NG):
        op("vector", lambda E, g=g: E.memset(Rre[g][:, 0:1], 1.0))
        op("vector", lambda E, g=g: E.memset(Rim[g][:, 0:1], 0.0))
        cp("vector", Rre[g][:, 1:2], small[:, C_COS + g:C_COS + g + 1])
        tsc("vector", Rim[g][:, 1:2], small[:, C_SIN + g:C_SIN + g + 1],
            -1.0, AL.mult)
        n = 2
        while n < T:
            pre, pim, t1 = smc(C_S1), smc(C_S2), smc(C_T1)
            a, b = Rre[g][:, n - 1:n], Rim[g][:, n - 1:n]
            c, d = Rre[g][:, 1:2], Rim[g][:, 1:2]
            tt("vector", pre, a, c, AL.mult)
            tt("vector", t1, b, d, AL.mult)
            tt("vector", pre, pre, t1, AL.subtract)
            tt("vector", pim, a, d, AL.mult)
            tt("vector", t1, b, c, AL.mult)
            tt("vector", pim, pim, t1, AL.add)
            w = min(n, T - n)
            sA, sB = pAT[:, 0:w], pAT[:, w:2 * w]
            tsc("vector", sA, Rre[g][:, 0:w], pre, AL.mult)
            tsc("vector", sB, Rim[g][:, 0:w], pim, AL.mult)
            tt("vector", Rre[g][:, n:n + w], sA, sB, AL.subtract)
            tsc("vector", sA, Rre[g][:, 0:w], pim, AL.mult)
            tsc("vector", sB, Rim[g][:, 0:w], pre, AL.mult)
            tt("vector", Rim[g][:, n:n + w], sA, sB, AL.add)
            n *= 2
    wait("vector", sS, mk_gam)
    for g in range(NG):
        gam = smc(C_GAM + g)
        tsc("vector", R_bf[:, (2 * g) * T:(2 * g + 1) * T], Rre[g], gam, AL.mult)
        tsc("vector", R_bf[:, (2 * g + 1) * T:(2 * g + 2) * T], Rim[g], gam,
            AL.mult)
    for g in range(NG):
        a, b = Rre[g][:, T - 1:T], Rim[g][:, T - 1:T]
        c, d = Rre[g][:, 1:2], Rim[g][:, 1:2]
        pre, pim, t1 = smc(28 + 2 * g), smc(28 + 2 * g + 1), smc(C_T1)
        tt("vector", pre, a, c, AL.mult)
        tt("vector", t1, b, d, AL.mult)
        tt("vector", pre, pre, t1, AL.subtract)
        tt("vector", pim, a, d, AL.mult)
        tt("vector", t1, b, c, AL.mult)
        tt("vector", pim, pim, t1, AL.add)
        tsc("vector", pim, pim, -1.0, AL.mult)
    milestone("vector")

    wait("vector", sT, mk_os)
    for q in range(4):
        cp("vector", row2[0:2, q * 512:(q + 1) * 512], ps[q][0:2, :])
    for q in range(4):
        cp("vector", row_sx2[0:1, q * 512:(q + 1) * 512], ps[4 + q][0:1, :])
    mk_ev = milestone("vector")
    wait("sync", sV, mk_ev)
    mk_s1 = dma("sync", sSm, row_S1[0:1, :], row2[1:2, :])

    fence(("tensor",))
    mk_ns = stats_pass(xNB.ap(), False)
    rowm = row2[0:1, :]
    tsc("vector", rowm, rowm, INV_D, AL.mult)
    mk_m = milestone("vector")
    wait("sync", sV, mk_m)
    bcast(rowm, m_bc)
    wait("scalar", sV, mk_m)
    mk_m2 = act(row_tmp[0:1, :], rowm, AF.Square, inc=sS)
    wait("vector", sS, mk_m2)
    stt("vector", row_sx2[0:1, :], row_sx2[0:1, :], INV_D, row_tmp[0:1, :],
        AL.mult, AL.subtract)
    tsc("vector", row_tmp[0:1, :], row_sx2[0:1, :], 1e-5, AL.add)
    mk_v = milestone("vector")
    wait("scalar", sV, mk_v)
    mk_sq = act(row_tmp[0:1, :], row_tmp[0:1, :], AF.Sqrt, inc=sS)
    wait("vector", sS, mk_sq)
    op("vector", lambda E: E.reciprocal(row_tmp[0:1, :], row_tmp[0:1, :]))
    mk_r = milestone("vector")
    wait("sync", sV, mk_r)
    bcast(row_tmp[0:1, :], r_bc)
    wait("vector", sBr, sBr.v)
    tsc("vector", rowm, row_sx2[0:1, :], 1e-5, AL.add)
    op("vector", lambda E: E.reciprocal(rowm, rowm))
    tt("vector", rowm, row_sx2[0:1, :], rowm, AL.mult)
    tsc("vector", rowm, rowm, 1e-6, AL.add)
    mk_v = milestone("vector")
    wait("scalar", sV, mk_v)
    mk_sq = act(rowm, rowm, AF.Sqrt, inc=sS)
    wait("vector", sS, mk_sq)
    op("vector", lambda E: E.reciprocal(rowm, rowm))
    tt("vector", rowm, rowm, row_tmp[0:1, :], AL.mult)
    mk_rr = milestone("vector")
    wait("sync", sV, mk_rr)
    bcast(rowm, rr1_bc)
    wait("vector", sT, mk_ns)
    wait("vector", sBr, sBr.v)
    for q in range(4):
        cp("vector", row2[0:1, q * 512:(q + 1) * 512], ps[q][0:1, :])
    for q in range(4):
        cp("vector", row_sx2[0:1, q * 512:(q + 1) * 512], ps[4 + q][0:1, :])
    mk_nbev = milestone("vector")
    onescol = arF(140, 1)
    op("vector", lambda E: E.memset(onescol, 1.0))
    mk_oc = milestone("vector")
    wait("tensor", sV, mk_oc)
    wait("tensor", sV, mk_nbev)
    mk_S2 = None
    for k in range(NK):
        mk_S2 = mm(ps[5][0:1, 0:1], sw_sb[:, 2 * k + 1:2 * k + 2], onescol,
                   k == 0, k == NK - 1, inc=sT if k == NK - 1 else None)
    wait("vector", sT, mk_S2)
    cp("vector", small[0:1, C_T1:C_T1 + 1], ps[5][0:1, 0:1])
    wait("vector", sSm, mk_s1)
    wait("vector", sBr, sBr.v)
    tsc("vector", row_tmp[0:1, :], m_bc[0:1, :], small[0:1, C_T1:C_T1 + 1],
        AL.mult)
    tt("vector", row_tmp[0:1, :], row_S1[0:1, :], row_tmp[0:1, :], AL.subtract)
    tt("vector", row_tmp[0:1, :], row_tmp[0:1, :], r_bc[0:1, :], AL.mult)
    tsc("vector", row_tmp[0:1, :], row_tmp[0:1, :], br_sb[0:1, 0:1], AL.add)
    tsc("vector", row_S1[0:1, :].bitcast(mybir.dt.uint8)[0:1, 0:T],
        row_tmp[0:1, :], 0.0, AL.is_gt)
    mk_msk = milestone("vector")
    wait("sync", sV, mk_msk)
    bcast(row_S1[0:1, :].bitcast(mybir.dt.uint8), mask_bc, u8=True)
    wait("vector", sBr, sBr.v)
    tsc("vector", rowm, rowm, INV_D, AL.mult)
    mk_m = milestone("vector")
    wait("scalar", sV, mk_m)
    mk_m2 = act(row_tmp[0:1, :], rowm, AF.Square, inc=sS)
    wait("vector", sS, mk_m2)
    stt("vector", row_sx2[0:1, :], row_sx2[0:1, :], INV_D, row_tmp[0:1, :],
        AL.mult, AL.subtract)
    tsc("vector", row_sx2[0:1, :], row_sx2[0:1, :], 1e-5, AL.add)
    mk_v = milestone("vector")
    wait("scalar", sV, mk_v)
    mk_sq = act(row_sx2[0:1, :], row_sx2[0:1, :], AF.Sqrt, inc=sS)
    wait("vector", sS, mk_sq)
    op("vector", lambda E: E.reciprocal(row_sx2[0:1, :], row_sx2[0:1, :]))
    mk_rnb = milestone("vector")
    mnb_bc = arF(120, T)
    rnb_bc = arF(128, T)
    wait("sync", sV, sV.v)
    bcast(rowm, mnb_bc)
    wait("sync", sV, mk_rnb)
    bcast(row_sx2[0:1, :], rnb_bc)


    pmark("P0stats_nb")
    wait("tensor", sBc, mk_bcat)
    fence(("tensor",))
    mk_bs = None
    for pl in range(4):
        for k in range(NK):
            mk_bs = mm(ps[3][:, pl:pl + 1],
                       bcat_sb[:, k * 512 + pl * 128:k * 512 + (pl + 1) * 128],
                       negones_bf[:, :], k == 0, k == NK - 1,
                       inc=sT if k == NK - 1 else None)
    wait("vector", sT, mk_bs)
    cp("vector", smc(C_BS + 0), ps[3][:, 0:1])
    cp("vector", smc(C_BS + 1), ps[3][:, 2:3])
    cp("vector", smc(C_BS + 2), ps[3][:, 1:2])
    mk_bsum = cp("vector", smc(C_BS + 3), ps[3][:, 3:4], inc=sV)

    upl = [arF(8 * i, T) for i in range(4)]
    xbf_s = [arB(136 + 2 * i, TS) for i in range(2)]
    rotA = arF(76, T)
    rotB = arF(128, T)

    def bu_pass(src, mbc, rbc, mk_ps_free):
        wait("tensor", sV, mk_ps_free)
        wait("vector", sBr, sBr.v)
        mk_free = None
        for qp in range(2):
            c0 = qp * TS
            mkt = {}
            for k in range(NK):
                sl = k % 3
                sb = k % 2
                wait("sync", sV, ring_v.get(sl))
                mk_x = dma("sync", sX[sl], xring[sl][:, 0:TS],
                           src[k * 128:(k + 1) * 128, c0:c0 + TS])
                wait("vector", sX[sl], mk_x)
                wait("vector", sT, bf_t.get(sb))
                ring_v[sl] = cp("vector", xbf_s[sb], xring[sl][:, 0:TS], inc=sV)
                wait("tensor", sV, ring_v[sl])
                for pl in range(4):
                    for q in range(2):
                        mkt[k] = mm(ps[pl * 2 + q][:, :],
                                    bcat_sb[:, k * 512 + pl * 128:
                                            k * 512 + (pl + 1) * 128],
                                    xbf_s[sb][:, q * 512:(q + 1) * 512],
                                    k == 0, k == NK - 1,
                                    inc=sT if (pl == 3 and q == 1) else None)
                bf_t[sb] = mkt[k]
            wait("vector", sT, mkt[NK - 1])
            perm = [0, 2, 1, 3]
            for di in range(4):
                pl = perm[di]
                for q in range(2):
                    cc = c0 + q * 512
                    cp("vector", upl[di][:, cc:cc + 512], ps[pl * 2 + q][:, :])
            mk_free = milestone("vector")
            wait("tensor", sV, mk_free)
            for di in range(4):
                stt("vector", upl[di][:, c0:c0 + TS], mbc[:, c0:c0 + TS],
                    smc(C_BS + di), upl[di][:, c0:c0 + TS], AL.mult, AL.add)
                tt("vector", upl[di][:, c0:c0 + TS], upl[di][:, c0:c0 + TS],
                   rbc[:, c0:c0 + TS], AL.mult)
        mk_v = milestone("vector")
        return mk_free, mk_v

    def rotate_scan(init_cols):
        for g in range(NG):
            Rr = R_bf[:, (2 * g) * T:(2 * g + 1) * T]
            Ri = R_bf[:, (2 * g + 1) * T:(2 * g + 2) * T]
            ur, ui = upl[2 * g], upl[2 * g + 1]
            tt("vector", rotA, ur, Rr, AL.mult)
            tt("vector", rotB, ui, Ri, AL.mult)
            tt("vector", rotA, rotA, rotB, AL.subtract)
            tt("vector", rotB, ui, Rr, AL.mult)
            tt("vector", ui, ur, Ri, AL.mult)
            tt("vector", rotB, rotB, ui, AL.add)
            magc = smc(C_MAG + g)
            for ri, dsrc in ((0, rotA), (1, rotB)):
                i = 2 * g + ri
                ini = init_cols[i] if init_cols else 0.0
                op("vector",
                   lambda E, o=upl[i], d1=dsrc, mc=magc, iv=ini:
                   E.tensor_tensor_scan(o, mc.broadcast_to([128, T]), d1,
                                        iv, AL.mult, AL.add),
                   inc=sV)

    pmark("P0bu_nb")
    wait("vector", sV, mk_bsum)
    mk_ps0 = milestone("vector")
    nb_free, nb_fv = bu_pass(xNB.ap(), mnb_bc, rnb_bc, mk_ps0)
    rotate_scan(None)
    for g in range(NG):
        pre, pim = smc(28 + 2 * g), smc(28 + 2 * g + 1)
        t1 = smc(C_T1)
        ge_r = upl[2 * g][:, T - 1:T]
        ge_i = upl[2 * g + 1][:, T - 1:T]
        gi_r, gi_i = smc(C_GI + 2 * g), smc(C_GI + 2 * g + 1)
        tsc("vector", gi_r, ge_r, pre, AL.mult)
        tsc("vector", t1, ge_i, pim, AL.mult)
        tt("vector", gi_r, gi_r, t1, AL.subtract)
        tsc("vector", gi_i, ge_r, pim, AL.mult)
        tsc("vector", t1, ge_i, pre, AL.mult)
        tt("vector", gi_i, gi_i, t1, AL.add)
        tsc("vector", gi_r, gi_r, cmask_sb[:, 0:1], AL.mult)
        tsc("vector", gi_i, gi_i, cmask_sb[:, 0:1], AL.mult)
    milestone("vector")

    pmark("P0bu_own")
    ow_free, ow_fv = bu_pass(xT.ap(), m_bc, r_bc, nb_free)

    def lru_tail_own():
        rotate_scan([smc(C_GI + i) for i in range(4)])
        for i in range(4):
            cp("vector", g_bf[:, i * T:(i + 1) * T], upl[i])
        milestone("vector")

    h_pl = [arF(4 * k, TS) if k < 8 else arF(64 + 4 * (k - 8), TS)
            for k in range(NK)]
    ff_pl = [arB(2 * j, TS) for j in range(FFC)]
    h1_pl = [arB(88 + 2 * k, TS) for k in range(NK)]
    h2_pl = h1_pl
    v_pl = [arB(32 + 2 * e, TS) for e in range(NK)]
    C_sb = arB(88, 4 * D)
    Hbf = [arB(104 + 2 * i, TS) for i in range(4)]
    t6a = arF(112, 512)
    t6b_ = [arF(114, 512), arF(116, 512)]
    t6g = arF(118, 512)
    t3 = arF(84, TS)
    xb3c = arF(132, TS)
    xb4 = arF(132, TS)
    hsq4 = [arB(140, TS), arB(138, TS)]
    t5 = [arF(136, 512), arF(138, 512)]
    rms2_bc = arF(128, TS)
    p6x = [arF(132, 512), arF(134, 512)]
    p6h = [arF(136, 512), arF(138, 512)]
    wring = [arB(120, NK * 128), arB(124, NK * 128)]
    upring = [arB(128, NK * 128), arB(132, NK * 128)]
    dring = [arB(120, 22 * 128), arB(125.5, 22 * 128)]

    def phase3(s):
        c0 = s * TS
        fence()
        mk_h1 = {}
        mk_t3u = {}
        p3buf = [(xring[0][:, 0:TS], sX[0]), (xring[1][:, 0:TS], sX[1]),
                 (xb3c, sX[3])]
        for k in range(NK):
            sl = k % 3
            buf, bsem = p3buf[sl]
            if k >= 3:
                wait("sync", sV, mk_t3u[k - 3])
            mk_x = dma("sync", bsem, buf, xT[k * 128:(k + 1) * 128, c0:c0 + TS])
            wait("vector", bsem, mk_x)
            if k >= 1:
                wait("vector", sS, mk_h1[k - 1])
            mk_t3u[k] = tt("vector", t3, buf, m_bc[:, c0:c0 + TS],
                           AL.subtract, inc=sV)
            mk_t = tt("vector", t3, t3, rr1_bc[:, c0:c0 + TS], AL.mult, inc=sV)
            wait("scalar", sV, mk_t)
            mk_h1[k] = act(h1_pl[k], t3, AF.Identity,
                           scale=vp[:, 16 + k:16 + k + 1], inc=sS)
        mk_w = {}
        mk_grp = {}
        mk_ev = {}
        for e in range(NK):
            slw = e % 2
            if e >= 2:
                wait("gpsimd", sT, mk_grp[e - 2])
            mk_w[e] = dma("gpsimd", sWv[slw],
                          wring[slw].rearrange("p (k c) -> p k c", k=NK),
                          wv_p[e].rearrange("k p c -> p k c"))
            wait("tensor", sWv[slw], mk_w[e])
            for q in range(2):
                idx = 2 * e + q
                bank = idx % 4
                if idx >= 4:
                    wait("tensor", sS, mk_ev[idx - 4])
                for k in range(NK):
                    if e == 0 and q == 0:
                        wait("tensor", sS, mk_h1[k])
                    mk_grp[e] = mm(ps[bank][:, :],
                                   wring[slw][:, k * 128:(k + 1) * 128],
                                   h1_pl[k][:, q * 512:(q + 1) * 512],
                                   k == 0, k == NK - 1,
                                   inc=sT if k == NK - 1 else None)
                wait("scalar", sT, mk_grp[e])
                mk_ev[idx] = act(v_pl[e][:, q * 512:(q + 1) * 512], ps[bank][:, :],
                                 AF.Identity, bias=vp[:, 48 + e:48 + e + 1],
                                 inc=sS)
        return mk_ev[2 * NK - 1]

    def phase4(s, mk_v_last):
        c0 = s * TS
        fence()
        wait("tensor", sS, mk_v_last)
        mk_w = {}
        mk_grp = {}
        mk_hq = {}
        mk_stq = {}
        mk_sq = {}
        mk_xc = {}
        for d in range(NK):
            slw = d % 2
            if d >= 2:
                wait("gpsimd", sT, mk_grp[d - 2])
            mk_w[d] = dma("gpsimd", sWv[slw],
                          wring[slw].rearrange("p (k c) -> p k c", k=NK),
                          wo_p[d].rearrange("k p c -> p k c"))
            if d >= 1:
                wait("sync", sV, mk_xc[d - 1])
            mk_x = dma("sync", sX[0], xb4, xT[d * 128:(d + 1) * 128, c0:c0 + TS])
            wait("tensor", sWv[slw], mk_w[d])
            for q in range(2):
                idx = 2 * d + q
                bank = idx % 4
                if idx >= 4:
                    wait("tensor", sV, mk_hq[idx - 4])
                for e in range(NK):
                    mk_grp[d] = mm(ps[bank][:, :],
                                   wring[slw][:, e * 128:(e + 1) * 128],
                                   v_pl[e][:, q * 512:(q + 1) * 512],
                                   e == 0, e == NK - 1,
                                   inc=sT if e == NK - 1 else None)
            wait("vector", sX[0], mk_x)
            mk_xc[d] = tt("vector", h_pl[d], xb4, m_bc[:, c0:c0 + TS],
                          AL.subtract, inc=sV)
            tt("vector", h_pl[d], h_pl[d], r_bc[:, c0:c0 + TS], AL.mult)
            wait("vector", sT, mk_grp[d])
            for q in range(2):
                bank = (2 * d + q) % 4
                mk_hq[2 * d + q] = tt("vector", h_pl[d][:, q * 512:(q + 1) * 512],
                                      h_pl[d][:, q * 512:(q + 1) * 512],
                                      ps[bank][:, :], AL.add, inc=sV)
            if d >= 2:
                dd = d - 2
                wait("tensor", sS, mk_sq[dd])
                for q in range(2):
                    mk_stq[dd] = mm(ps[4 + q][0:1, :], ones_bf[:, :],
                                    hsq4[dd % 2][:, q * 512:(q + 1) * 512],
                                    dd == 0, dd == NK - 1,
                                    inc=sT if q == 1 else None)
            wait("scalar", sV, mk_hq[2 * d + 1])
            if d >= 2:
                wait("scalar", sT, mk_stq[d - 2])
            mk_sq[d] = act(hsq4[d % 2], h_pl[d], AF.Square, inc=sS)
            if d == NK - 1:
                for dd in (d - 1, d):
                    wait("tensor", sS, mk_sq[dd])
                    for q in range(2):
                        mk_stq[dd] = mm(ps[4 + q][0:1, :], ones_bf[:, :],
                                        hsq4[dd % 2][:, q * 512:(q + 1) * 512],
                                        dd == 0, dd == NK - 1,
                                        inc=sT if q == 1 else None)
        wait("vector", sT, mk_stq[NK - 1])
        cp("vector", rms2_bc[0:1, 0:512], ps[4][0:1, :])
        cp("vector", rms2_bc[0:1, 512:1024], ps[5][0:1, :])
        ts2("vector", rms2_bc[0:1, :], rms2_bc[0:1, :], INV_D, 1e-6,
            AL.mult, AL.add)
        mk_v = milestone("vector")
        wait("scalar", sV, mk_v)
        mk_sq = act(rms2_bc[0:1, :], rms2_bc[0:1, :], AF.Sqrt, inc=sS)
        wait("vector", sS, mk_sq)
        op("vector", lambda E: E.reciprocal(rms2_bc[0:1, :], rms2_bc[0:1, :]))
        mk_row = milestone("vector")
        wait("sync", sV, mk_row)
        mk_r2bc = bcast(rms2_bc[0:1, :], rms2_bc, qn=2)
        wait("vector", sBr, mk_r2bc)
        mk_psfree4 = milestone("vector")
        h2tmp = [xb4, arF(136, TS)]
        mk_h2 = {}
        mk_tt = {}
        for i, d in enumerate(range(NK - 1, -1, -1)):
            sl = i % 2
            if i >= 2:
                wait("vector", sS, mk_h2[d + 2])
            if d == 3:
                wait("vector", sO, sO.v)
            mk_tt[d] = tt("vector", h2tmp[sl], h_pl[d], rms2_bc[:, :], AL.mult,
                          inc=sV)
            wait("scalar", sV, mk_tt[d])
            mk_h2[d] = act(h2_pl[d], h2tmp[sl], AF.Identity,
                           scale=vp[:, 32 + d:32 + d + 1], inc=sS)
            wait("sync", sV, mk_tt[d])
            dma("sync", sO, hsp[d * 128:(d + 1) * 128, c0:c0 + TS], h_pl[d])
        return mk_h2, sO.v, mk_psfree4, mk_tt[0]

    def phase5(s, mk_h2, mk_psfree4, mk_h2tt):
        fence(skip=(("tensor", "scalar"), ("tensor", "vector"),
                    ("gpsimd", "scalar"), ("gpsimd", "vector")))
        wait("tensor", sV, mk_psfree4)
        mk_wu = {}
        mk_grp = {}
        mk_ff = {}
        mk_si = {}
        for j in range(FFC):
            slw = j % 2
            if j >= 2:
                wait("gpsimd", sT, mk_grp[j - 2])
            if j < 2:
                wait("gpsimd", sV, mk_h2tt)
                wait("gpsimd", sS, mk_h2[0])
            mk_wg = dma("gpsimd", sWv[slw],
                        wring[slw].rearrange("p (k c) -> p k c", k=NK),
                        gate_p[j].rearrange("k p c -> p k c"))
            mk_wu[j] = dma("gpsimd", sUp[slw],
                           upring[slw].rearrange("p (k c) -> p k c", k=NK),
                           up_p[j].rearrange("k p c -> p k c"))
            wait("tensor", sWv[slw], mk_wg)
            wait("tensor", sUp[slw], mk_wu[j])
            bank0 = (j % 2) * 4
            for q in range(2):
                if j >= 2:
                    wait("tensor", sV, mk_ff[(j - 2, q)])
                    wait("tensor", sS, mk_si[(j - 2, q)])
                for i, k in enumerate(range(NK - 1, -1, -1)):
                    if j == 0 and q == 0:
                        wait("tensor", sS, mk_h2[k])
                    mm(ps[bank0 + q][:, :], wring[slw][:, k * 128:(k + 1) * 128],
                       h2_pl[k][:, q * 512:(q + 1) * 512], i == 0, i == NK - 1)
                for i, k in enumerate(range(NK - 1, -1, -1)):
                    mk_grp[j] = mm(ps[bank0 + 2 + q][:, :],
                                   upring[slw][:, k * 128:(k + 1) * 128],
                                   h2_pl[k][:, q * 512:(q + 1) * 512],
                                   i == 0, i == NK - 1,
                                   inc=sT if i == NK - 1 else None)
                wait("scalar", sT, mk_grp[j])
                if j >= 1:
                    wait("scalar", sV, mk_ff.get((j - 1, q), 0) or 0)
                mk_si[(j, q)] = act(t5[q], ps[bank0 + q][:, :], AF.Silu, inc=sS)
                wait("vector", sS, mk_si[(j, q)])
                mk_ff[(j, q)] = tt("vector", ff_pl[j][:, q * 512:(q + 1) * 512],
                                   t5[q], ps[bank0 + 2 + q][:, :], AL.mult,
                                   inc=sV)
        return mk_ff[(FFC - 1, 1)]

    def phase6(s, mk_ff_last, spill_mark):
        c0 = s * TS
        fence()
        mk_C = dma("gpsimd", sBc, C_sb.rearrange("p (q c) -> p q c", q=4),
                   Ccat.ap().rearrange("q p c -> p q c"))
        scr = arF(112, TS)
        for g in range(NG):
            gr = g_bf[:, (2 * g) * T + c0:(2 * g) * T + c0 + TS]
            gi = g_bf[:, (2 * g + 1) * T + c0:(2 * g + 1) * T + c0 + TS]
            rr = R_bf[:, (2 * g) * T + c0:(2 * g) * T + c0 + TS]
            ri = R_bf[:, (2 * g + 1) * T + c0:(2 * g + 1) * T + c0 + TS]
            tt("vector", scr, gr, rr, AL.mult)
            tt("vector", Hbf[2 * g], gi, ri, AL.mult)
            tt("vector", Hbf[2 * g], scr, Hbf[2 * g], AL.add)
            tt("vector", scr, gi, rr, AL.mult)
            tt("vector", Hbf[2 * g + 1], gr, ri, AL.mult)
            tt("vector", Hbf[2 * g + 1], Hbf[2 * g + 1], scr, AL.subtract)
        mk_Hc = milestone("vector")

        wait("sync", sO, spill_mark)
        wait("tensor", sV, mk_ff_last)
        mk_grp = {}
        mk_half = {}
        mk_ep = {}
        mk_g = {}
        mk_gsil = {}
        for d in range(NK):
            if d >= 1:
                wait("gpsimd", sT, mk_half[(d - 1, 1, 0)])
            mk_w0 = dma("gpsimd", sDn[0],
                        dring[0].rearrange("p (j c) -> p j c", j=22),
                        down_p[d, 0:22].rearrange("j p c -> p j c"))
            if d >= 1:
                wait("gpsimd", sT, mk_half[(d - 1, 1, 1)])
            mk_w1 = dma("gpsimd", sDn[1],
                        dring[1].rearrange("p (j c) -> p j c", j=22),
                        down_p[d, 22:44].rearrange("j p c -> p j c"))
            if d >= 1:
                wait("sync", sV, mk_ep[2 * d - 1])
                wait("gpsimd", sV, mk_ep[2 * d - 1])
            mk_h = {}
            mk_x = {}
            for q in range(2):
                mk_h[q] = dma("gpsimd", sH[q], p6h[q],
                              hsp[d * 128:(d + 1) * 128,
                                  c0 + q * 512:c0 + (q + 1) * 512])
                mk_x[q] = dma("sync", sX[q], p6x[q],
                              xT[d * 128:(d + 1) * 128,
                                 c0 + q * 512:c0 + (q + 1) * 512])
            for q in range(2):
                idx = 2 * d + q
                bankd = idx % 4
                banky = 4 + idx % 4
                if idx >= 4:
                    wait("tensor", sV, mk_ep[idx - 4])
                if q == 0:
                    wait("tensor", sDn[0], mk_w0)
                for j in range(22):
                    mk_half[(d, q, 0)] = mm(
                        ps[bankd][:, :], dring[0][:, j * 128:(j + 1) * 128],
                        ff_pl[j][:, q * 512:(q + 1) * 512], j == 0, False,
                        inc=sT if j == 21 else None)
                if q == 0:
                    wait("tensor", sDn[1], mk_w1)
                for j in range(22):
                    mk_half[(d, q, 1)] = mm(
                        ps[bankd][:, :], dring[1][:, j * 128:(j + 1) * 128],
                        ff_pl[22 + j][:, q * 512:(q + 1) * 512], False, j == 21,
                        inc=sT if j == 21 else None)
                if d == 0 and q == 0:
                    wait("tensor", sV, mk_Hc)
                    wait("tensor", sBc, mk_C)
                for pi in range(4):
                    g = pi % 2
                    ri = pi // 2
                    mk_grp[idx] = mm(
                        ps[banky][:, :],
                        C_sb[:, pi * D + d * 128:pi * D + (d + 1) * 128],
                        Hbf[2 * g + ri][:, q * 512:(q + 1) * 512],
                        pi == 0, pi == 3, inc=sT if pi == 3 else None)
                cc = c0 + q * 512
                tb = t6b_[idx % 2]
                wait("vector", sT, mk_grp[idx])
                wait("vector", sX[q], mk_x[q])
                wait("vector", sH[q], mk_h[q])
                wait("vector", sOut[idx % 2], sOut[idx % 2].v)
                tt("vector", t6a, p6x[q], m_bc[:, cc:cc + 512], AL.subtract)
                tt("vector", t6a, t6a, r_bc[:, cc:cc + 512], AL.mult)
                tt("vector", tb, p6h[q], ps[bankd][:, :], AL.add)
                if idx >= 1:
                    wait("vector", sS, mk_gsil[idx - 1])
                mk_sel = sel(t6g, mask_bc[:, cc:cc + 512], tb, t6a, inc=sV)
                wait("scalar", sV, mk_sel)
                mk_gsil[idx] = act(t6g, t6g, AF.Silu, inc=sS)
                wait("vector", sS, mk_gsil[idx])
                stt("vector", tb, t6a, vp[:, d:d + 1], ps[banky][:, :],
                    AL.mult, AL.add)
                mk_g[idx] = tt("vector", tb, tb, t6g, AL.mult, inc=sV)
                mk_ep[idx] = tt("vector", tb, tb, p6x[q], AL.add, inc=sV)
                wait("sync", sV, mk_ep[idx])
                dma("sync", sOut[idx % 2],
                    out_d[d * 128:(d + 1) * 128, cc:cc + 512], tb)

    for s in range(NSUB):
        pmark("P3_s%d" % s)
        mk_v_last = phase3(s)
        if s == 0:
            lru_tail_own()
        pmark("P4_s%d" % s)
        mk_h2, spill_mark, mk_pf4, mk_h2tt = phase4(s, mk_v_last)
        pmark("P5_s%d" % s)
        mk_ff_last = phase5(s, mk_h2, mk_pf4, mk_h2tt)
        pmark("P6_s%d" % s)
        phase6(s, mk_ff_last, spill_mark)

    em("sync", lambda E, v=sO.v: E.wait_ge(sO.h, v))
    em("sync", lambda E, v=sOut[0].v: E.wait_ge(sOut[0].h, v))
    em("sync", lambda E, v=sOut[1].v: E.wait_ge(sOut[1].h, v))

    with nc.Block() as block:
        @block.tensor
        def _(E):
            for f in prog["tensor"]:
                f(E)

        @block.vector
        def _(E):
            for f in prog["vector"]:
                f(E)

        @block.scalar
        def _(E):
            for f in prog["scalar"]:
                f(E)

        @block.sync
        def _(E):
            for f in prog["sync"]:
                f(E)

        @block.gpsimd
        def _(E):
            for f in prog["gpsimd"]:
                f(E)

    es.close()
    return nc


_CACHE = {}


def _pack_inputs(inputs):
    f32 = np.float32
    x = np.asarray(inputs["L0_attn_x"], f32)
    wv = np.asarray(inputs["L0_ffn_wv"], f32)
    wo = np.asarray(inputs["L0_ffn_wo"], f32)
    wg = np.asarray(inputs["L0_ffn_w_gate"], f32)
    wu = np.asarray(inputs["L0_ffn_w_up"], f32)
    wd = np.asarray(inputs["L0_ffn_w_down"], f32)
    Bre = np.asarray(inputs["L0_attn_B_re"], f32)
    Bim = np.asarray(inputs["L0_attn_B_im"], f32)
    Cre = np.asarray(inputs["L0_attn_C_re"], f32)
    Cim = np.asarray(inputs["L0_attn_C_im"], f32)

    wv_p = np.ascontiguousarray(wv.reshape(NK, 128, NK, 128).transpose(2, 0, 1, 3))
    wo_p = np.ascontiguousarray(wo.reshape(NK, 128, NK, 128).transpose(2, 0, 1, 3))
    gate_p = np.ascontiguousarray(wg.reshape(NK, 128, FFC, 128).transpose(2, 0, 1, 3))
    up_p = np.ascontiguousarray(wu.reshape(NK, 128, FFC, 128).transpose(2, 0, 1, 3))
    down_p = np.ascontiguousarray(wd.reshape(FFC, 128, NK, 128).transpose(2, 0, 1, 3))
    Bcat = np.ascontiguousarray(
        np.concatenate([Bre.T, Bim.T], axis=1).reshape(NK, 128, 512))
    Ccat = np.ascontiguousarray(np.concatenate(
        [Cre.T.reshape(2, 128, D), Cim.T.reshape(2, 128, D)], axis=0))

    def col16(v):
        return np.ascontiguousarray(np.asarray(v, f32).reshape(NK, 128).T)

    vparams = np.zeros((128, 64), f32)
    vparams[:, 0:16] = col16(inputs["L0_attn_D"])
    vparams[:, 16:32] = col16(inputs["L0_ffn_norm1_w"])
    vparams[:, 32:48] = col16(inputs["L0_ffn_norm2_w"])
    vparams[:, 48:64] = col16(inputs["L0_ffn_bv"])

    stats_w = np.zeros((128, 32), f32)
    wr = col16(inputs["L0_ffn_w_router"])
    for k in range(NK):
        stats_w[:, 2 * k] = 1.0
        stats_w[:, 2 * k + 1] = wr[:, k]

    lru = np.zeros((128, 4), f32)
    lru[:, 0:2] = np.asarray(inputs["L0_attn_nu_log"], f32).reshape(2, 128).T
    lru[:, 2:4] = np.asarray(inputs["L0_attn_theta_log"], f32).reshape(2, 128).T

    br = np.asarray(inputs["L0_ffn_b_router"], f32).reshape(1, 1)

    shared = dict(wv_p=wv_p, wo_p=wo_p, gate_p=gate_p, up_p=up_p, down_p=down_p,
                  Bcat=Bcat, Ccat=Ccat, vparams=vparams, stats_w=stats_w,
                  lru_in=lru, brouter=br)
    in_maps = []
    for c in range(8):
        b, half = c // 2, c % 2
        xs = np.ascontiguousarray(x[b, half * T:(half + 1) * T, :].T)
        xnb = np.ascontiguousarray(x[b, 0:T, :].T) if half == 1 else xs
        cm = np.full((128, 1), float(half), f32)
        m = dict(shared)
        m.update(xT=xs, xNB=xnb, cmask=cm)
        in_maps.append(m)
    return in_maps


def kernel(**inputs):
    in_maps = _pack_inputs(inputs)
    if "nc" not in _CACHE:
        _CACHE["nc"] = build_nc()
    nc = _CACHE["nc"]
    res = run_bass_kernel_spmd(nc, in_maps, core_ids=list(range(8)))
    out = np.empty((4, 4096, D), np.float32)
    for c in range(8):
        b, half = c // 2, c % 2
        out[b, half * T:(half + 1) * T, :] = res.results[c]["out"].T
    return out


# version 11
# speedup vs baseline: 1.0335x; 1.0071x over previous
import numpy as np
import contextlib
import concourse.bass as bass
import concourse.mybir as mybir
from concourse.bass_utils import run_bass_kernel_spmd

F32 = mybir.dt.float32
BF16 = mybir.dt.bfloat16
AL = mybir.AluOpType
AF = mybir.ActivationFunctionType

D = 2048
NK = 16
T = 2048
TS = 1024
NSUB = 2
FFC = 44
NG = 2
INV_D = float(np.float32(1.0 / D))
TWOPI = float(np.float32(2 * np.pi))
PI = float(np.pi)

ENGINES = ("tensor", "vector", "scalar", "sync", "gpsimd")


class SemC:
    def __init__(self, h):
        self.h = h
        self.v = 0


def build_nc():
    nc = bass.Bass("TRN2", target_bir_lowering=False)

    xT = nc.dram_tensor("xT", [D, T], F32, kind="ExternalInput")
    xNB = nc.dram_tensor("xNB", [D, T], F32, kind="ExternalInput")
    wv_p = nc.dram_tensor("wv_p", [NK, NK, 128, 128], F32, kind="ExternalInput")
    wo_p = nc.dram_tensor("wo_p", [NK, NK, 128, 128], F32, kind="ExternalInput")
    gate_p = nc.dram_tensor("gate_p", [FFC, NK, 128, 128], F32, kind="ExternalInput")
    up_p = nc.dram_tensor("up_p", [FFC, NK, 128, 128], F32, kind="ExternalInput")
    down_p = nc.dram_tensor("down_p", [NK, FFC, 128, 128], F32, kind="ExternalInput")
    Bcat = nc.dram_tensor("Bcat", [NK, 128, 512], F32, kind="ExternalInput")
    Ccat = nc.dram_tensor("Ccat", [4, 128, D], F32, kind="ExternalInput")
    vparams = nc.dram_tensor("vparams", [128, 64], F32, kind="ExternalInput")
    stats_w = nc.dram_tensor("stats_w", [128, 32], F32, kind="ExternalInput")
    lru_in = nc.dram_tensor("lru_in", [128, 4], F32, kind="ExternalInput")
    brouter = nc.dram_tensor("brouter", [1, 1], F32, kind="ExternalInput")
    cmask = nc.dram_tensor("cmask", [128, 1], F32, kind="ExternalInput")
    out_d = nc.dram_tensor("out", [D, T], F32, kind="ExternalOutput")
    hsp = nc.dram_tensor("hspill", [D, T], F32)
    bscr = nc.dram_tensor("bscr", [1, T], F32)
    bscr8 = nc.dram_tensor("bscr8", [1, T], mybir.dt.uint8)

    m_bc = nc.alloc_sbuf_tensor("m_bc", [128, T], F32)
    r_bc = nc.alloc_sbuf_tensor("r_bc", [128, T], F32)
    rr1_bc = nc.alloc_sbuf_tensor("rr1_bc", [128, T], F32)
    mask_bc = nc.alloc_sbuf_tensor("mask_bc", [128, T], mybir.dt.uint8)
    g_bf = nc.alloc_sbuf_tensor("g_bf", [128, 4 * T], BF16)
    R_bf = nc.alloc_sbuf_tensor("R_bf", [128, 4 * T], BF16)
    vp = nc.alloc_sbuf_tensor("vp", [128, 64], F32)
    small = nc.alloc_sbuf_tensor("small", [128, 32], F32)
    ones1 = nc.alloc_sbuf_tensor("ones1", [1, 128], F32)
    ones_bf = nc.alloc_sbuf_tensor("ones_bf", [128, 1], BF16)
    negones_bf = nc.alloc_sbuf_tensor("negones_bf", [128, 1], BF16)
    cmask_sb = nc.alloc_sbuf_tensor("cmask_sb", [128, 1], F32)
    br_sb = nc.alloc_sbuf_tensor("br_sb", [1, 1], F32)
    sw_sb = nc.alloc_sbuf_tensor("sw_sb", [128, 32], F32)

    AR_F32 = 36352
    arena = nc.alloc_sbuf_tensor("arena", [128, AR_F32], F32)
    arena_bf = arena.ap().bitcast(BF16)

    def arF(off_kb, cols):
        o = int(off_kb * 256)
        assert o + cols <= AR_F32, (off_kb, cols)
        return arena[:, o:o + cols]

    def arB(off_kb, cols):
        o = int(off_kb * 512)
        assert o + cols <= 2 * AR_F32, (off_kb, cols)
        return arena_bf[:, o:o + cols]

    ps = [nc.alloc_psum_tensor("ps%d" % i, [128, 512], F32) for i in range(8)]

    es = contextlib.ExitStack()
    def mksem(name):
        return SemC(es.enter_context(nc.semaphore(name)))
    sSm = mksem("sSm")
    sBc = mksem("sBc")
    sX = [mksem("sX0"), mksem("sX1"), mksem("sX2"), mksem("sX3")]
    sH = [mksem("sH0"), mksem("sH1")]
    sWv = [mksem("sWv0"), mksem("sWv1")]
    sUp = [mksem("sUp0"), mksem("sUp1")]
    sDn = [mksem("sDn0"), mksem("sDn1")]
    sO = mksem("sO")
    sBr = mksem("sBr")
    sOut = [mksem("sOut0"), mksem("sOut1")]
    sT = mksem("sT")
    sV = mksem("sV")
    sS = mksem("sS")
    sG = mksem("sG")
    ENG_SEM = {"tensor": sT, "vector": sV, "scalar": sS, "gpsimd": sG}
    DMA_SEMS = (sSm, sBc, sX[0], sX[1], sX[2], sX[3], sH[0], sH[1],
                sWv[0], sWv[1], sUp[0], sUp[1], sDn[0], sDn[1], sO, sBr,
                sOut[0], sOut[1])

    prog = {e: [] for e in ENGINES}
    phase_marks = []

    def pmark(name):
        phase_marks.append((name, len(prog["tensor"])))
    build_nc.phase_marks = phase_marks
    lastw = {}

    dirty = {e: False for e in ENGINES}

    def em(e, fn):
        prog[e].append(fn)

    def drain(e):
        if dirty[e]:
            prog[e].append(lambda E: E.drain())
            dirty[e] = False

    def wait(e, s, v=None):
        v = s.v if v is None else v
        if not v or v <= 0:
            return
        key = (e, id(s))
        if lastw.get(key, -1) >= v:
            return
        lastw[key] = v
        em(e, lambda E, h=s.h, vv=v: E.wait_ge(h, vv))

    def op(e, fn, inc=None, n=1, is_dma=False):
        if e in ("vector", "scalar", "gpsimd") and not is_dma:
            drain(e)
            dirty[e] = True
        if inc is None:
            em(e, lambda E, f=fn: f(E))
            return None
        inc.v += n
        mk = inc.v
        em(e, lambda E, f=fn, h=inc.h, nn=n: f(E).then_inc(h, nn))
        return mk

    def dma(queue, sem, out_ap, in_ap):
        return op(queue, lambda E, o=out_ap, i=in_ap: E.dma_start(out=o, in_=i),
                  inc=sem, n=16, is_dma=True)

    def mm(out_ap, lhs_ap, rhs_ap, start, stop, inc=None):
        return op("tensor",
                  lambda E, o=out_ap, l=lhs_ap, r=rhs_ap, a=start, b=stop:
                  E.matmul(o, l, r, start=a, stop=b), inc=inc)

    def tt(e, out_ap, a_ap, b_ap, alu, inc=None):
        return op(e, lambda E, o=out_ap, a=a_ap, b=b_ap, al=alu:
                  E.tensor_tensor(o, a, b, al), inc=inc)

    def tsc(e, out_ap, a_ap, s1, alu, inc=None):
        return op(e, lambda E, o=out_ap, a=a_ap, s=s1, al=alu:
                  E.tensor_single_scalar(o, a, s, al), inc=inc)

    def ts2(e, out_ap, a_ap, s1, s2, op0, op1, inc=None):
        return op(e, lambda E, o=out_ap, a=a_ap, x=s1, y=s2, o0=op0, o1=op1:
                  E.tensor_scalar(o, a, x, y, o0, o1), inc=inc)

    def stt(e, out_ap, a_ap, sc, b_ap, op0, op1, inc=None):
        return op(e, lambda E, o=out_ap, a=a_ap, s=sc, b=b_ap, o0=op0, o1=op1:
                  E.scalar_tensor_tensor(o, a, s, b, o0, o1), inc=inc)

    def act(out_ap, in_ap, func, bias=0.0, scale=1.0, inc=None):
        return op("scalar",
                  lambda E, o=out_ap, i=in_ap, f=func, b=bias, s=scale:
                  E.activation(o, i, f, bias=b, scale=s), inc=inc)

    def cp(e, out_ap, in_ap, inc=None):
        return op(e, lambda E, o=out_ap, i=in_ap: E.tensor_copy(o, i), inc=inc)

    def sel(out_ap, m_ap, a_ap, b_ap, inc=None):
        return op("vector", lambda E, o=out_ap, m=m_ap, a=a_ap, b=b_ap:
                  E.select(o, m, a, b, add_drain=True), inc=inc)

    def milestone(e):
        s = ENG_SEM[e]
        s.v += 1
        mk = s.v
        em(e, lambda E, h=s.h: E.drain().then_inc(h, 1))
        dirty[e] = False
        return mk

    def fence(writers=ENGINES, skip=()):
        marks = {e: milestone(e) for e in ("tensor", "vector", "scalar", "gpsimd")}
        for e in writers:
            for pe, mk in marks.items():
                if pe != e and (e, pe) not in skip:
                    wait(e, ENG_SEM[pe], mk)
            for dsem in DMA_SEMS:
                wait(e, dsem, dsem.v)

    C_NU, C_TH, C_MAG, C_COS, C_SIN = 0, 2, 4, 6, 8
    C_S1, C_S2 = 10, 12
    C_GI = 16
    C_BS = 20
    C_T1 = 24
    C_GAM = 26

    def sm(c, w=2):
        return small[:, c:c + w]

    def smc(c):
        return small[:, c:c + 1]

    pmark("P0a")
    dma("sync", sSm, vp[:, :], vparams.ap())
    dma("sync", sSm, small[:, 0:4], lru_in.ap())
    dma("sync", sSm, sw_sb[:, :], stats_w.ap())
    dma("sync", sSm, br_sb[:, :], brouter.ap())
    mk_small = dma("sync", sSm, cmask_sb[:, :], cmask.ap())
    op("vector", lambda E: E.memset(ones1[:, :], 1.0))
    op("vector", lambda E: E.memset(ones_bf[:, :], 1.0))
    mk_ones = op("vector", lambda E: E.memset(negones_bf[:, :], -1.0), inc=sV)

    bcat_sb = arB(96, NK * 512)
    mk_bcat = dma("gpsimd", sBc, bcat_sb.rearrange("p (k c) -> p k c", k=NK),
                  Bcat.ap().rearrange("k p c -> p k c"))

    xring = [arF(64, T), arF(72, T), arF(112, T), arF(120, T)]
    xbf = [arB(80 + 4 * i, T) for i in range(2)]
    xsq = [arB(88 + 4 * i, T) for i in range(2)]
    ring_v = {}
    ring_t = {}
    bf_t = {}
    xbf_sc = {}
    xbf_tm = {}
    xsq_tm = {}
    row2 = arF(32, T)
    row_S1 = arF(40, T)
    row_sx2 = arF(48, T)
    row_tmp = arF(56, T)

    def stats_pass(src, own):
        mkv = {}
        mkt = {}
        mk_sq = {}
        for k in range(NK):
            sl = k % 4
            sb = k % 2
            wait("sync", sV, ring_v.get(sl))
            wait("sync", sT, ring_t.get(sl))
            mk_x = dma("sync", sX[sl], xring[sl], src[k * 128:(k + 1) * 128, :])
            wait("vector", sX[sl], mk_x)
            wait("vector", sS, xbf_sc.get(sb))
            wait("vector", sT, xbf_tm.get(sb))
            mkv[k] = ring_v[sl] = cp("vector", xbf[sb], xring[sl], inc=sV)
            wait("scalar", sV, mkv[k])
            wait("scalar", sT, xsq_tm.get(sb))
            mk_sq[k] = xbf_sc[sb] = act(xsq[sb], xbf[sb], AF.Square, inc=sS)
            if own:
                wait("tensor", sX[sl], mk_x)
                for q in range(4):
                    ring_t[sl] = mm(ps[q][0:2, :], sw_sb[:, 2 * k:2 * k + 2],
                                    xring[sl][:, q * 512:(q + 1) * 512],
                                    k == 0, k == NK - 1,
                                    inc=sT if q == 3 else None)
            else:
                wait("tensor", sV, mkv[k])
                for q in range(4):
                    mk_nbx = mm(ps[q][0:1, :], ones_bf[:, :],
                                xbf[sb][:, q * 512:(q + 1) * 512],
                                k == 0, k == NK - 1,
                                inc=sT if q == 3 else None)
                xbf_tm[sb] = mk_nbx
            wait("tensor", sS, mk_sq[k])
            for q in range(4):
                mkt[k] = mm(ps[4 + q][0:1, :], ones_bf[:, :],
                            xsq[sb][:, q * 512:(q + 1) * 512], k == 0, k == NK - 1,
                            inc=sT if q == 3 else None)
            xsq_tm[sb] = mkt[k]
        return mkt[NK - 1]

    def bcast(row_ap, dst, qn=4, u8=False):
        n = qn * 512
        scr = (bscr8 if u8 else bscr).ap()[0:1, 0:n]
        wait("sync", sBr, sBr.v)
        mk1 = dma("sync", sBr, scr, row_ap[:, 0:n])
        wait("sync", sBr, mk1)
        return dma("sync", sBr, dst[:, 0:n], scr.partition_broadcast(128))

    pmark("P0stats_own")
    wait("tensor", sSm, mk_small)
    mk_os = stats_pass(xT.ap(), True)
    wait("scalar", sSm, mk_small)
    act(sm(C_S1), sm(C_NU), AF.Abs)
    act(sm(C_S1), sm(C_S1), AF.Exp, scale=-1.0)
    act(sm(C_S1), sm(C_S1), AF.Ln, bias=1.0)
    mk = op("scalar", lambda E: E.copy(sm(C_S2), sm(C_NU)), inc=sS)
    wait("vector", sS, mk)
    wait("vector", sV, mk_ones)
    tsc("vector", sm(C_S2), sm(C_S2), 0.0, AL.max)
    mk_v = tt("vector", sm(C_MAG), sm(C_S2), sm(C_S1), AL.add, inc=sV)
    wait("scalar", sV, mk_v)
    act(sm(C_MAG), sm(C_MAG), AF.Exp)
    mk_mag = act(sm(C_MAG), sm(C_MAG), AF.Exp, scale=-1.0, inc=sS)
    wait("vector", sS, mk_mag)
    tt("vector", sm(C_GAM), sm(C_MAG), sm(C_MAG), AL.mult)
    ts2("vector", sm(C_GAM), sm(C_GAM), -1.0, 1.0, AL.mult, AL.add)
    mk_v = tsc("vector", sm(C_GAM), sm(C_GAM), 0.0, AL.max, inc=sV)
    wait("scalar", sV, mk_v)
    act(sm(C_GAM), sm(C_GAM), AF.Sqrt)
    mk_gam = act(sm(C_GAM), sm(C_GAM), AF.Sqrt, inc=sS)
    act(sm(C_TH), sm(C_TH), AF.Exp)
    mk_th = op("scalar", lambda E: E.copy(sm(C_S1), sm(C_TH)), inc=sS)
    wait("vector", sS, mk_th)
    for _ in range(8):
        tsc("vector", sm(C_S2), sm(C_S1), PI, AL.is_gt)
        mk_v = stt("vector", sm(C_S1), sm(C_S2), -TWOPI, sm(C_S1),
                   AL.mult, AL.add, inc=sV)
    wait("scalar", sV, mk_v)
    act(sm(C_SIN), sm(C_S1), AF.Sin)
    mk_sh = act(sm(C_S2), sm(C_S1), AF.Sin, scale=0.5, inc=sS)
    wait("vector", sS, mk_sh)
    tt("vector", sm(C_COS), sm(C_S2), sm(C_S2), AL.mult)
    mk_cos = ts2("vector", sm(C_COS), sm(C_COS), -2.0, 1.0, AL.mult, AL.add,
                 inc=sV)

    Rre = [arF(16 * g, T) for g in range(NG)]
    Rim = [arF(8 + 16 * g, T) for g in range(NG)]
    pAT = arF(32, T)

    wait("vector", sV, mk_cos)
    for g in range(NG):
        op("vector", lambda E, g=g: E.memset(Rre[g][:, 0:1], 1.0))
        op("vector", lambda E, g=g: E.memset(Rim[g][:, 0:1], 0.0))
        cp("vector", Rre[g][:, 1:2], small[:, C_COS + g:C_COS + g + 1])
        tsc("vector", Rim[g][:, 1:2], small[:, C_SIN + g:C_SIN + g + 1],
            -1.0, AL.mult)
        n = 2
        while n < T:
            pre, pim, t1 = smc(C_S1), smc(C_S2), smc(C_T1)
            a, b = Rre[g][:, n - 1:n], Rim[g][:, n - 1:n]
            c, d = Rre[g][:, 1:2], Rim[g][:, 1:2]
            tt("vector", pre, a, c, AL.mult)
            tt("vector", t1, b, d, AL.mult)
            tt("vector", pre, pre, t1, AL.subtract)
            tt("vector", pim, a, d, AL.mult)
            tt("vector", t1, b, c, AL.mult)
            tt("vector", pim, pim, t1, AL.add)
            w = min(n, T - n)
            sA, sB = pAT[:, 0:w], pAT[:, w:2 * w]
            tsc("vector", sA, Rre[g][:, 0:w], pre, AL.mult)
            tsc("vector", sB, Rim[g][:, 0:w], pim, AL.mult)
            tt("vector", Rre[g][:, n:n + w], sA, sB, AL.subtract)
            tsc("vector", sA, Rre[g][:, 0:w], pim, AL.mult)
            tsc("vector", sB, Rim[g][:, 0:w], pre, AL.mult)
            tt("vector", Rim[g][:, n:n + w], sA, sB, AL.add)
            n *= 2
    wait("vector", sS, mk_gam)
    for g in range(NG):
        gam = smc(C_GAM + g)
        tsc("vector", R_bf[:, (2 * g) * T:(2 * g + 1) * T], Rre[g], gam, AL.mult)
        tsc("vector", R_bf[:, (2 * g + 1) * T:(2 * g + 2) * T], Rim[g], gam,
            AL.mult)
    for g in range(NG):
        a, b = Rre[g][:, T - 1:T], Rim[g][:, T - 1:T]
        c, d = Rre[g][:, 1:2], Rim[g][:, 1:2]
        pre, pim, t1 = smc(28 + 2 * g), smc(28 + 2 * g + 1), smc(C_T1)
        tt("vector", pre, a, c, AL.mult)
        tt("vector", t1, b, d, AL.mult)
        tt("vector", pre, pre, t1, AL.subtract)
        tt("vector", pim, a, d, AL.mult)
        tt("vector", t1, b, c, AL.mult)
        tt("vector", pim, pim, t1, AL.add)
        tsc("vector", pim, pim, -1.0, AL.mult)
    milestone("vector")

    wait("vector", sT, mk_os)
    for q in range(4):
        cp("vector", row2[0:2, q * 512:(q + 1) * 512], ps[q][0:2, :])
    for q in range(4):
        cp("vector", row_sx2[0:1, q * 512:(q + 1) * 512], ps[4 + q][0:1, :])
    mk_ev = milestone("vector")
    wait("sync", sV, mk_ev)
    mk_s1 = dma("sync", sSm, row_S1[0:1, :], row2[1:2, :])

    fence(("tensor",))
    mk_ns = stats_pass(xNB.ap(), False)
    rowm = row2[0:1, :]
    tsc("vector", rowm, rowm, INV_D, AL.mult)
    mk_m = milestone("vector")
    wait("sync", sV, mk_m)
    bcast(rowm, m_bc)
    wait("scalar", sV, mk_m)
    mk_m2 = act(row_tmp[0:1, :], rowm, AF.Square, inc=sS)
    wait("vector", sS, mk_m2)
    stt("vector", row_sx2[0:1, :], row_sx2[0:1, :], INV_D, row_tmp[0:1, :],
        AL.mult, AL.subtract)
    tsc("vector", row_tmp[0:1, :], row_sx2[0:1, :], 1e-5, AL.add)
    mk_v = milestone("vector")
    wait("scalar", sV, mk_v)
    mk_sq = act(row_tmp[0:1, :], row_tmp[0:1, :], AF.Sqrt, inc=sS)
    wait("vector", sS, mk_sq)
    op("vector", lambda E: E.reciprocal(row_tmp[0:1, :], row_tmp[0:1, :]))
    mk_r = milestone("vector")
    wait("sync", sV, mk_r)
    bcast(row_tmp[0:1, :], r_bc)
    wait("vector", sBr, sBr.v)
    tsc("vector", rowm, row_sx2[0:1, :], 1e-5, AL.add)
    op("vector", lambda E: E.reciprocal(rowm, rowm))
    tt("vector", rowm, row_sx2[0:1, :], rowm, AL.mult)
    tsc("vector", rowm, rowm, 1e-6, AL.add)
    mk_v = milestone("vector")
    wait("scalar", sV, mk_v)
    mk_sq = act(rowm, rowm, AF.Sqrt, inc=sS)
    wait("vector", sS, mk_sq)
    op("vector", lambda E: E.reciprocal(rowm, rowm))
    tt("vector", rowm, rowm, row_tmp[0:1, :], AL.mult)
    mk_rr = milestone("vector")
    wait("sync", sV, mk_rr)
    bcast(rowm, rr1_bc)
    wait("vector", sT, mk_ns)
    wait("vector", sBr, sBr.v)
    for q in range(4):
        cp("vector", row2[0:1, q * 512:(q + 1) * 512], ps[q][0:1, :])
    for q in range(4):
        cp("vector", row_sx2[0:1, q * 512:(q + 1) * 512], ps[4 + q][0:1, :])
    mk_nbev = milestone("vector")
    onescol = arF(140, 1)
    op("vector", lambda E: E.memset(onescol, 1.0))
    mk_oc = milestone("vector")
    wait("tensor", sV, mk_oc)
    wait("tensor", sV, mk_nbev)
    mk_S2 = None
    for k in range(NK):
        mk_S2 = mm(ps[5][0:1, 0:1], sw_sb[:, 2 * k + 1:2 * k + 2], onescol,
                   k == 0, k == NK - 1, inc=sT if k == NK - 1 else None)
    wait("vector", sT, mk_S2)
    cp("vector", small[0:1, C_T1:C_T1 + 1], ps[5][0:1, 0:1])
    wait("vector", sSm, mk_s1)
    wait("vector", sBr, sBr.v)
    tsc("vector", row_tmp[0:1, :], m_bc[0:1, :], small[0:1, C_T1:C_T1 + 1],
        AL.mult)
    tt("vector", row_tmp[0:1, :], row_S1[0:1, :], row_tmp[0:1, :], AL.subtract)
    tt("vector", row_tmp[0:1, :], row_tmp[0:1, :], r_bc[0:1, :], AL.mult)
    tsc("vector", row_tmp[0:1, :], row_tmp[0:1, :], br_sb[0:1, 0:1], AL.add)
    tsc("vector", row_S1[0:1, :].bitcast(mybir.dt.uint8)[0:1, 0:T],
        row_tmp[0:1, :], 0.0, AL.is_gt)
    mk_msk = milestone("vector")
    wait("sync", sV, mk_msk)
    bcast(row_S1[0:1, :].bitcast(mybir.dt.uint8), mask_bc, u8=True)
    wait("vector", sBr, sBr.v)
    tsc("vector", rowm, rowm, INV_D, AL.mult)
    mk_m = milestone("vector")
    wait("scalar", sV, mk_m)
    mk_m2 = act(row_tmp[0:1, :], rowm, AF.Square, inc=sS)
    wait("vector", sS, mk_m2)
    stt("vector", row_sx2[0:1, :], row_sx2[0:1, :], INV_D, row_tmp[0:1, :],
        AL.mult, AL.subtract)
    tsc("vector", row_sx2[0:1, :], row_sx2[0:1, :], 1e-5, AL.add)
    mk_v = milestone("vector")
    wait("scalar", sV, mk_v)
    mk_sq = act(row_sx2[0:1, :], row_sx2[0:1, :], AF.Sqrt, inc=sS)
    wait("vector", sS, mk_sq)
    op("vector", lambda E: E.reciprocal(row_sx2[0:1, :], row_sx2[0:1, :]))
    mk_rnb = milestone("vector")
    mnb_bc = arF(120, T)
    rnb_bc = arF(128, T)
    wait("sync", sV, sV.v)
    bcast(rowm, mnb_bc)
    wait("sync", sV, mk_rnb)
    bcast(row_sx2[0:1, :], rnb_bc)


    pmark("P0stats_nb")
    wait("tensor", sBc, mk_bcat)
    fence(("tensor",))
    mk_bs = None
    for pl in range(4):
        for k in range(NK):
            mk_bs = mm(ps[3][:, pl:pl + 1],
                       bcat_sb[:, k * 512 + pl * 128:k * 512 + (pl + 1) * 128],
                       negones_bf[:, :], k == 0, k == NK - 1,
                       inc=sT if k == NK - 1 else None)
    wait("vector", sT, mk_bs)
    cp("vector", smc(C_BS + 0), ps[3][:, 0:1])
    cp("vector", smc(C_BS + 1), ps[3][:, 2:3])
    cp("vector", smc(C_BS + 2), ps[3][:, 1:2])
    mk_bsum = cp("vector", smc(C_BS + 3), ps[3][:, 3:4], inc=sV)

    upl = [arF(8 * i, T) for i in range(4)]
    xbf_s = [arB(136 + 2 * i, TS) for i in range(2)]
    rotA = arF(76, T)
    rotB = arF(128, T)

    def bu_pass(src, mbc, rbc, mk_ps_free):
        wait("tensor", sV, mk_ps_free)
        wait("vector", sBr, sBr.v)
        mk_free = None
        for qp in range(2):
            c0 = qp * TS
            mkt = {}
            for k in range(NK):
                sl = k % 3
                sb = k % 2
                wait("sync", sV, ring_v.get(sl))
                mk_x = dma("sync", sX[sl], xring[sl][:, 0:TS],
                           src[k * 128:(k + 1) * 128, c0:c0 + TS])
                wait("vector", sX[sl], mk_x)
                wait("vector", sT, bf_t.get(sb))
                ring_v[sl] = cp("vector", xbf_s[sb], xring[sl][:, 0:TS], inc=sV)
                wait("tensor", sV, ring_v[sl])
                for pl in range(4):
                    for q in range(2):
                        mkt[k] = mm(ps[pl * 2 + q][:, :],
                                    bcat_sb[:, k * 512 + pl * 128:
                                            k * 512 + (pl + 1) * 128],
                                    xbf_s[sb][:, q * 512:(q + 1) * 512],
                                    k == 0, k == NK - 1,
                                    inc=sT if (pl == 3 and q == 1) else None)
                bf_t[sb] = mkt[k]
            wait("vector", sT, mkt[NK - 1])
            perm = [0, 2, 1, 3]
            for di in range(4):
                pl = perm[di]
                for q in range(2):
                    cc = c0 + q * 512
                    cp("vector", upl[di][:, cc:cc + 512], ps[pl * 2 + q][:, :])
            mk_free = milestone("vector")
            wait("tensor", sV, mk_free)
            for di in range(4):
                stt("vector", upl[di][:, c0:c0 + TS], mbc[:, c0:c0 + TS],
                    smc(C_BS + di), upl[di][:, c0:c0 + TS], AL.mult, AL.add)
                tt("vector", upl[di][:, c0:c0 + TS], upl[di][:, c0:c0 + TS],
                   rbc[:, c0:c0 + TS], AL.mult)
        mk_v = milestone("vector")
        return mk_free, mk_v

    def rotate_scan(init_cols):
        for g in range(NG):
            Rr = R_bf[:, (2 * g) * T:(2 * g + 1) * T]
            Ri = R_bf[:, (2 * g + 1) * T:(2 * g + 2) * T]
            ur, ui = upl[2 * g], upl[2 * g + 1]
            tt("vector", rotA, ur, Rr, AL.mult)
            tt("vector", rotB, ui, Ri, AL.mult)
            tt("vector", rotA, rotA, rotB, AL.subtract)
            tt("vector", rotB, ui, Rr, AL.mult)
            tt("vector", ui, ur, Ri, AL.mult)
            tt("vector", rotB, rotB, ui, AL.add)
            magc = smc(C_MAG + g)
            for ri, dsrc in ((0, rotA), (1, rotB)):
                i = 2 * g + ri
                ini = init_cols[i] if init_cols else 0.0
                op("vector",
                   lambda E, o=upl[i], d1=dsrc, mc=magc, iv=ini:
                   E.tensor_tensor_scan(o, mc.broadcast_to([128, T]), d1,
                                        iv, AL.mult, AL.add),
                   inc=sV)

    pmark("P0bu_nb")
    wait("vector", sV, mk_bsum)
    mk_ps0 = milestone("vector")
    nb_free, nb_fv = bu_pass(xNB.ap(), mnb_bc, rnb_bc, mk_ps0)
    rotate_scan(None)
    for g in range(NG):
        pre, pim = smc(28 + 2 * g), smc(28 + 2 * g + 1)
        t1 = smc(C_T1)
        ge_r = upl[2 * g][:, T - 1:T]
        ge_i = upl[2 * g + 1][:, T - 1:T]
        gi_r, gi_i = smc(C_GI + 2 * g), smc(C_GI + 2 * g + 1)
        tsc("vector", gi_r, ge_r, pre, AL.mult)
        tsc("vector", t1, ge_i, pim, AL.mult)
        tt("vector", gi_r, gi_r, t1, AL.subtract)
        tsc("vector", gi_i, ge_r, pim, AL.mult)
        tsc("vector", t1, ge_i, pre, AL.mult)
        tt("vector", gi_i, gi_i, t1, AL.add)
        tsc("vector", gi_r, gi_r, cmask_sb[:, 0:1], AL.mult)
        tsc("vector", gi_i, gi_i, cmask_sb[:, 0:1], AL.mult)
    milestone("vector")

    pmark("P0bu_own")
    ow_free, ow_fv = bu_pass(xT.ap(), m_bc, r_bc, nb_free)

    def lru_tail_own():
        rotate_scan([smc(C_GI + i) for i in range(4)])
        for i in range(4):
            cp("vector", g_bf[:, i * T:(i + 1) * T], upl[i])
        milestone("vector")

    h_pl = [arF(4 * k, TS) if k < 8 else arF(64 + 4 * (k - 8), TS)
            for k in range(NK)]
    ff_pl = [arB(2 * j, TS) for j in range(FFC)]
    h1_pl = [arB(88 + 2 * k, TS) for k in range(NK)]
    h2_pl = h1_pl
    v_pl = [arB(32 + 2 * e, TS) for e in range(NK)]
    C_sb = arB(88, 4 * D)
    Hbf = [arB(104 + 2 * i, TS) for i in range(4)]
    t6a = arF(112, 512)
    t6b_ = [arF(114, 512), arF(116, 512)]
    t6g = arF(118, 512)
    t3 = arF(84, TS)
    xb3c = arF(132, TS)
    xb4 = arF(132, TS)
    hsq4 = [arB(140, TS), arB(138, TS)]
    t5 = [arF(136, 512), arF(138, 512)]
    rms2_bc = arF(128, TS)
    p6x = [arF(132, 512), arF(134, 512)]
    p6h = [arF(136, 512), arF(138, 512)]
    wring = [arB(120, NK * 128), arB(124, NK * 128)]
    upring = [arB(128, NK * 128), arB(132, NK * 128)]
    dring = [arB(120, 22 * 128), arB(125.5, 22 * 128)]

    def phase3(s):
        c0 = s * TS
        fence()
        mk_h1 = {}
        mk_t3u = {}
        p3buf = [(xring[0][:, 0:TS], sX[0]), (xring[1][:, 0:TS], sX[1]),
                 (xb3c, sX[3])]
        for k in range(NK):
            sl = k % 3
            buf, bsem = p3buf[sl]
            if k >= 3:
                wait("sync", sV, mk_t3u[k - 3])
            mk_x = dma("sync", bsem, buf, xT[k * 128:(k + 1) * 128, c0:c0 + TS])
            wait("vector", bsem, mk_x)
            if k >= 1:
                wait("vector", sS, mk_h1[k - 1])
            mk_t3u[k] = tt("vector", t3, buf, m_bc[:, c0:c0 + TS],
                           AL.subtract, inc=sV)
            mk_t = tt("vector", t3, t3, rr1_bc[:, c0:c0 + TS], AL.mult, inc=sV)
            wait("scalar", sV, mk_t)
            mk_h1[k] = act(h1_pl[k], t3, AF.Identity,
                           scale=vp[:, 16 + k:16 + k + 1], inc=sS)
        mk_w = {}
        mk_grp = {}
        mk_ev = {}
        for e in range(NK):
            slw = e % 2
            if e >= 2:
                wait("gpsimd", sT, mk_grp[e - 2])
            mk_w[e] = dma("gpsimd", sWv[slw],
                          wring[slw].rearrange("p (k c) -> p k c", k=NK),
                          wv_p[e].rearrange("k p c -> p k c"))
            wait("tensor", sWv[slw], mk_w[e])
            for q in range(2):
                idx = 2 * e + q
                bank = idx % 4
                if idx >= 4:
                    wait("tensor", sS, mk_ev[idx - 4])
                for k in range(NK):
                    if e == 0 and q == 0:
                        wait("tensor", sS, mk_h1[k])
                    mk_grp[e] = mm(ps[bank][:, :],
                                   wring[slw][:, k * 128:(k + 1) * 128],
                                   h1_pl[k][:, q * 512:(q + 1) * 512],
                                   k == 0, k == NK - 1,
                                   inc=sT if k == NK - 1 else None)
                wait("scalar", sT, mk_grp[e])
                mk_ev[idx] = act(v_pl[e][:, q * 512:(q + 1) * 512], ps[bank][:, :],
                                 AF.Identity, bias=vp[:, 48 + e:48 + e + 1],
                                 inc=sS)
        return mk_ev[2 * NK - 1]

    def phase4(s, mk_v_last):
        c0 = s * TS
        fence()
        wait("tensor", sS, mk_v_last)
        mk_w = {}
        mk_grp = {}
        mk_hq = {}
        mk_stq = {}
        mk_sq = {}
        mk_xc = {}
        for d in range(NK):
            slw = d % 2
            if d >= 2:
                wait("gpsimd", sT, mk_grp[d - 2])
            mk_w[d] = dma("gpsimd", sWv[slw],
                          wring[slw].rearrange("p (k c) -> p k c", k=NK),
                          wo_p[d].rearrange("k p c -> p k c"))
            if d >= 1:
                wait("sync", sV, mk_xc[d - 1])
            mk_x = dma("sync", sX[0], xb4, xT[d * 128:(d + 1) * 128, c0:c0 + TS])
            wait("tensor", sWv[slw], mk_w[d])
            for q in range(2):
                idx = 2 * d + q
                bank = idx % 4
                if idx >= 4:
                    wait("tensor", sV, mk_hq[idx - 4])
                for e in range(NK):
                    mk_grp[d] = mm(ps[bank][:, :],
                                   wring[slw][:, e * 128:(e + 1) * 128],
                                   v_pl[e][:, q * 512:(q + 1) * 512],
                                   e == 0, e == NK - 1,
                                   inc=sT if e == NK - 1 else None)
            wait("vector", sX[0], mk_x)
            mk_xc[d] = tt("vector", h_pl[d], xb4, m_bc[:, c0:c0 + TS],
                          AL.subtract, inc=sV)
            tt("vector", h_pl[d], h_pl[d], r_bc[:, c0:c0 + TS], AL.mult)
            wait("vector", sT, mk_grp[d])
            for q in range(2):
                bank = (2 * d + q) % 4
                mk_hq[2 * d + q] = tt("vector", h_pl[d][:, q * 512:(q + 1) * 512],
                                      h_pl[d][:, q * 512:(q + 1) * 512],
                                      ps[bank][:, :], AL.add, inc=sV)
            if d >= 2:
                dd = d - 2
                wait("tensor", sS, mk_sq[dd])
                for q in range(2):
                    mk_stq[dd] = mm(ps[4 + q][0:1, :], ones_bf[:, :],
                                    hsq4[dd % 2][:, q * 512:(q + 1) * 512],
                                    dd == 0, dd == NK - 1,
                                    inc=sT if q == 1 else None)
            wait("scalar", sV, mk_hq[2 * d + 1])
            if d >= 2:
                wait("scalar", sT, mk_stq[d - 2])
            mk_sq[d] = act(hsq4[d % 2], h_pl[d], AF.Square, inc=sS)
            if d == NK - 1:
                for dd in (d - 1, d):
                    wait("tensor", sS, mk_sq[dd])
                    for q in range(2):
                        mk_stq[dd] = mm(ps[4 + q][0:1, :], ones_bf[:, :],
                                        hsq4[dd % 2][:, q * 512:(q + 1) * 512],
                                        dd == 0, dd == NK - 1,
                                        inc=sT if q == 1 else None)
        wait("vector", sT, mk_stq[NK - 1])
        cp("vector", rms2_bc[0:1, 0:512], ps[4][0:1, :])
        cp("vector", rms2_bc[0:1, 512:1024], ps[5][0:1, :])
        ts2("vector", rms2_bc[0:1, :], rms2_bc[0:1, :], INV_D, 1e-6,
            AL.mult, AL.add)
        mk_v = milestone("vector")
        wait("scalar", sV, mk_v)
        mk_sq = act(rms2_bc[0:1, :], rms2_bc[0:1, :], AF.Sqrt, inc=sS)
        wait("vector", sS, mk_sq)
        op("vector", lambda E: E.reciprocal(rms2_bc[0:1, :], rms2_bc[0:1, :]))
        mk_row = milestone("vector")
        wait("sync", sV, mk_row)
        mk_r2bc = bcast(rms2_bc[0:1, :], rms2_bc, qn=2)
        wait("vector", sBr, mk_r2bc)
        mk_psfree4 = milestone("vector")
        h2tmp = [xb4, arF(136, TS)]
        mk_h2 = {}
        mk_tt = {}
        for i, d in enumerate(range(NK - 1, -1, -1)):
            sl = i % 2
            if i >= 2:
                wait("vector", sS, mk_h2[d + 2])
            if d == 3:
                wait("vector", sO, sO.v)
            mk_tt[d] = tt("vector", h2tmp[sl], h_pl[d], rms2_bc[:, :], AL.mult,
                          inc=sV)
            wait("scalar", sV, mk_tt[d])
            mk_h2[d] = act(h2_pl[d], h2tmp[sl], AF.Identity,
                           scale=vp[:, 32 + d:32 + d + 1], inc=sS)
            wait("sync", sV, mk_tt[d])
            dma("sync", sO, hsp[d * 128:(d + 1) * 128, c0:c0 + TS], h_pl[d])
        return mk_h2, sO.v, mk_psfree4, mk_tt[0]

    def phase5(s, mk_h2, mk_psfree4, mk_h2tt):
        fence(skip=(("tensor", "scalar"), ("tensor", "vector"),
                    ("gpsimd", "scalar"), ("gpsimd", "vector")))
        wait("tensor", sV, mk_psfree4)
        mk_wu = {}
        mk_grp = {}
        mk_ff = {}
        mk_si = {}
        for j in range(FFC):
            slw = j % 2
            if j >= 2:
                wait("gpsimd", sT, mk_grp[j - 2])
            if j < 2:
                wait("gpsimd", sV, mk_h2tt)
                wait("gpsimd", sS, mk_h2[0])
            mk_wg = dma("gpsimd", sWv[slw],
                        wring[slw].rearrange("p (k c) -> p k c", k=NK),
                        gate_p[j].rearrange("k p c -> p k c"))
            mk_wu[j] = dma("gpsimd", sUp[slw],
                           upring[slw].rearrange("p (k c) -> p k c", k=NK),
                           up_p[j].rearrange("k p c -> p k c"))
            wait("tensor", sWv[slw], mk_wg)
            wait("tensor", sUp[slw], mk_wu[j])
            bank0 = (j % 2) * 4
            for q in range(2):
                if j >= 2:
                    wait("tensor", sV, mk_ff[(j - 2, q)])
                    wait("tensor", sS, mk_si[(j - 2, q)])
                for i, k in enumerate(range(NK - 1, -1, -1)):
                    if j == 0 and q == 0:
                        wait("tensor", sS, mk_h2[k])
                    mm(ps[bank0 + q][:, :], wring[slw][:, k * 128:(k + 1) * 128],
                       h2_pl[k][:, q * 512:(q + 1) * 512], i == 0, i == NK - 1)
                for i, k in enumerate(range(NK - 1, -1, -1)):
                    mk_grp[j] = mm(ps[bank0 + 2 + q][:, :],
                                   upring[slw][:, k * 128:(k + 1) * 128],
                                   h2_pl[k][:, q * 512:(q + 1) * 512],
                                   i == 0, i == NK - 1,
                                   inc=sT if i == NK - 1 else None)
                wait("scalar", sT, mk_grp[j])
                if j >= 1:
                    wait("scalar", sV, mk_ff.get((j - 1, q), 0) or 0)
                mk_si[(j, q)] = act(t5[q], ps[bank0 + q][:, :], AF.Silu, inc=sS)
                wait("vector", sS, mk_si[(j, q)])
                mk_ff[(j, q)] = tt("vector", ff_pl[j][:, q * 512:(q + 1) * 512],
                                   t5[q], ps[bank0 + 2 + q][:, :], AL.mult,
                                   inc=sV)
        return mk_ff[(FFC - 1, 1)]

    def phase6(s, mk_ff_last, spill_mark):
        c0 = s * TS
        fence()
        mk_C = dma("gpsimd", sBc, C_sb.rearrange("p (q c) -> p q c", q=4),
                   Ccat.ap().rearrange("q p c -> p q c"))
        scr = arF(112, TS)
        for g in range(NG):
            gr = g_bf[:, (2 * g) * T + c0:(2 * g) * T + c0 + TS]
            gi = g_bf[:, (2 * g + 1) * T + c0:(2 * g + 1) * T + c0 + TS]
            rr = R_bf[:, (2 * g) * T + c0:(2 * g) * T + c0 + TS]
            ri = R_bf[:, (2 * g + 1) * T + c0:(2 * g + 1) * T + c0 + TS]
            tt("vector", scr, gr, rr, AL.mult)
            tt("vector", Hbf[2 * g], gi, ri, AL.mult)
            tt("vector", Hbf[2 * g], scr, Hbf[2 * g], AL.add)
            tt("vector", scr, gi, rr, AL.mult)
            tt("vector", Hbf[2 * g + 1], gr, ri, AL.mult)
            tt("vector", Hbf[2 * g + 1], Hbf[2 * g + 1], scr, AL.subtract)
        mk_Hc = milestone("vector")

        wait("sync", sO, spill_mark)
        wait("tensor", sV, mk_ff_last)
        mk_grp = {}
        mk_half = {}
        mk_ep = {}
        mk_g = {}
        mk_gsil = {}
        for d in range(NK):
            if d >= 1:
                wait("gpsimd", sT, mk_half[(d - 1, 1, 0)])
            mk_w0 = dma("gpsimd", sDn[0],
                        dring[0].rearrange("p (j c) -> p j c", j=22),
                        down_p[d, 0:22].rearrange("j p c -> p j c"))
            if d >= 1:
                wait("gpsimd", sT, mk_half[(d - 1, 1, 1)])
            mk_w1 = dma("gpsimd", sDn[1],
                        dring[1].rearrange("p (j c) -> p j c", j=22),
                        down_p[d, 22:44].rearrange("j p c -> p j c"))
            if d >= 1:
                wait("sync", sV, mk_ep[2 * d - 1])
            mk_h = {}
            mk_x = {}
            for q in range(2):
                mk_h[q] = dma("sync", sH[q], p6h[q],
                              hsp[d * 128:(d + 1) * 128,
                                  c0 + q * 512:c0 + (q + 1) * 512])
                mk_x[q] = dma("sync", sX[q], p6x[q],
                              xT[d * 128:(d + 1) * 128,
                                 c0 + q * 512:c0 + (q + 1) * 512])
            for q in range(2):
                idx = 2 * d + q
                bankd = idx % 4
                banky = 4 + idx % 4
                if idx >= 4:
                    wait("tensor", sV, mk_ep[idx - 4])
                if q == 0:
                    wait("tensor", sDn[0], mk_w0)
                for j in range(22):
                    mk_half[(d, q, 0)] = mm(
                        ps[bankd][:, :], dring[0][:, j * 128:(j + 1) * 128],
                        ff_pl[j][:, q * 512:(q + 1) * 512], j == 0, False,
                        inc=sT if j == 21 else None)
                if q == 0:
                    wait("tensor", sDn[1], mk_w1)
                for j in range(22):
                    mk_half[(d, q, 1)] = mm(
                        ps[bankd][:, :], dring[1][:, j * 128:(j + 1) * 128],
                        ff_pl[22 + j][:, q * 512:(q + 1) * 512], False, j == 21,
                        inc=sT if j == 21 else None)
                if d == 0 and q == 0:
                    wait("tensor", sV, mk_Hc)
                    wait("tensor", sBc, mk_C)
                for pi in range(4):
                    g = pi % 2
                    ri = pi // 2
                    mk_grp[idx] = mm(
                        ps[banky][:, :],
                        C_sb[:, pi * D + d * 128:pi * D + (d + 1) * 128],
                        Hbf[2 * g + ri][:, q * 512:(q + 1) * 512],
                        pi == 0, pi == 3, inc=sT if pi == 3 else None)
                cc = c0 + q * 512
                tb = t6b_[idx % 2]
                wait("vector", sT, mk_grp[idx])
                wait("vector", sX[q], mk_x[q])
                wait("vector", sH[q], mk_h[q])
                wait("vector", sOut[idx % 2], sOut[idx % 2].v)
                tt("vector", t6a, p6x[q], m_bc[:, cc:cc + 512], AL.subtract)
                tt("vector", t6a, t6a, r_bc[:, cc:cc + 512], AL.mult)
                tt("vector", tb, p6h[q], ps[bankd][:, :], AL.add)
                if idx >= 1:
                    wait("vector", sS, mk_gsil[idx - 1])
                mk_sel = sel(t6g, mask_bc[:, cc:cc + 512], tb, t6a, inc=sV)
                wait("scalar", sV, mk_sel)
                mk_gsil[idx] = act(t6g, t6g, AF.Silu, inc=sS)
                wait("vector", sS, mk_gsil[idx])
                stt("vector", tb, t6a, vp[:, d:d + 1], ps[banky][:, :],
                    AL.mult, AL.add)
                mk_g[idx] = tt("vector", tb, tb, t6g, AL.mult, inc=sV)
                mk_ep[idx] = tt("vector", tb, tb, p6x[q], AL.add, inc=sV)
                wait("sync", sV, mk_ep[idx])
                dma("sync", sOut[idx % 2],
                    out_d[d * 128:(d + 1) * 128, cc:cc + 512], tb)

    for s in range(NSUB):
        pmark("P3_s%d" % s)
        mk_v_last = phase3(s)
        if s == 0:
            lru_tail_own()
        pmark("P4_s%d" % s)
        mk_h2, spill_mark, mk_pf4, mk_h2tt = phase4(s, mk_v_last)
        pmark("P5_s%d" % s)
        mk_ff_last = phase5(s, mk_h2, mk_pf4, mk_h2tt)
        pmark("P6_s%d" % s)
        phase6(s, mk_ff_last, spill_mark)

    em("sync", lambda E, v=sO.v: E.wait_ge(sO.h, v))
    em("sync", lambda E, v=sOut[0].v: E.wait_ge(sOut[0].h, v))
    em("sync", lambda E, v=sOut[1].v: E.wait_ge(sOut[1].h, v))

    with nc.Block() as block:
        @block.tensor
        def _(E):
            for f in prog["tensor"]:
                f(E)

        @block.vector
        def _(E):
            for f in prog["vector"]:
                f(E)

        @block.scalar
        def _(E):
            for f in prog["scalar"]:
                f(E)

        @block.sync
        def _(E):
            for f in prog["sync"]:
                f(E)

        @block.gpsimd
        def _(E):
            for f in prog["gpsimd"]:
                f(E)

    es.close()
    return nc


_CACHE = {}


def _pack_inputs(inputs):
    f32 = np.float32
    x = np.asarray(inputs["L0_attn_x"], f32)
    wv = np.asarray(inputs["L0_ffn_wv"], f32)
    wo = np.asarray(inputs["L0_ffn_wo"], f32)
    wg = np.asarray(inputs["L0_ffn_w_gate"], f32)
    wu = np.asarray(inputs["L0_ffn_w_up"], f32)
    wd = np.asarray(inputs["L0_ffn_w_down"], f32)
    Bre = np.asarray(inputs["L0_attn_B_re"], f32)
    Bim = np.asarray(inputs["L0_attn_B_im"], f32)
    Cre = np.asarray(inputs["L0_attn_C_re"], f32)
    Cim = np.asarray(inputs["L0_attn_C_im"], f32)

    wv_p = np.ascontiguousarray(wv.reshape(NK, 128, NK, 128).transpose(2, 0, 1, 3))
    wo_p = np.ascontiguousarray(wo.reshape(NK, 128, NK, 128).transpose(2, 0, 1, 3))
    gate_p = np.ascontiguousarray(wg.reshape(NK, 128, FFC, 128).transpose(2, 0, 1, 3))
    up_p = np.ascontiguousarray(wu.reshape(NK, 128, FFC, 128).transpose(2, 0, 1, 3))
    down_p = np.ascontiguousarray(wd.reshape(FFC, 128, NK, 128).transpose(2, 0, 1, 3))
    Bcat = np.ascontiguousarray(
        np.concatenate([Bre.T, Bim.T], axis=1).reshape(NK, 128, 512))
    Ccat = np.ascontiguousarray(np.concatenate(
        [Cre.T.reshape(2, 128, D), Cim.T.reshape(2, 128, D)], axis=0))

    def col16(v):
        return np.ascontiguousarray(np.asarray(v, f32).reshape(NK, 128).T)

    vparams = np.zeros((128, 64), f32)
    vparams[:, 0:16] = col16(inputs["L0_attn_D"])
    vparams[:, 16:32] = col16(inputs["L0_ffn_norm1_w"])
    vparams[:, 32:48] = col16(inputs["L0_ffn_norm2_w"])
    vparams[:, 48:64] = col16(inputs["L0_ffn_bv"])

    stats_w = np.zeros((128, 32), f32)
    wr = col16(inputs["L0_ffn_w_router"])
    for k in range(NK):
        stats_w[:, 2 * k] = 1.0
        stats_w[:, 2 * k + 1] = wr[:, k]

    lru = np.zeros((128, 4), f32)
    lru[:, 0:2] = np.asarray(inputs["L0_attn_nu_log"], f32).reshape(2, 128).T
    lru[:, 2:4] = np.asarray(inputs["L0_attn_theta_log"], f32).reshape(2, 128).T

    br = np.asarray(inputs["L0_ffn_b_router"], f32).reshape(1, 1)

    shared = dict(wv_p=wv_p, wo_p=wo_p, gate_p=gate_p, up_p=up_p, down_p=down_p,
                  Bcat=Bcat, Ccat=Ccat, vparams=vparams, stats_w=stats_w,
                  lru_in=lru, brouter=br)
    in_maps = []
    for c in range(8):
        b, half = c // 2, c % 2
        xs = np.ascontiguousarray(x[b, half * T:(half + 1) * T, :].T)
        xnb = np.ascontiguousarray(x[b, 0:T, :].T) if half == 1 else xs
        cm = np.full((128, 1), float(half), f32)
        m = dict(shared)
        m.update(xT=xs, xNB=xnb, cmask=cm)
        in_maps.append(m)
    return in_maps


def kernel(**inputs):
    in_maps = _pack_inputs(inputs)
    if "nc" not in _CACHE:
        _CACHE["nc"] = build_nc()
    nc = _CACHE["nc"]
    res = run_bass_kernel_spmd(nc, in_maps, core_ids=list(range(8)))
    out = np.empty((4, 4096, D), np.float32)
    for c in range(8):
        b, half = c // 2, c % 2
        out[b, half * T:(half + 1) * T, :] = res.results[c]["out"].T
    return out
